# version 02 (final)
# baseline (speedup 1.0000x reference)
import math
import numpy as np
from contextlib import ExitStack

import concourse.bass as bass
import concourse.mybir as mybir
from concourse.bass import IndirectOffsetOnAxis, ds
from concourse.bass_utils import run_bass_kernel_spmd

F32 = mybir.dt.float32
BF16 = mybir.dt.bfloat16
I32 = mybir.dt.int32
U32 = mybir.dt.uint32
U8 = mybir.dt.uint8
AF = mybir.ActivationFunctionType
ALU = mybir.AluOpType
AX = mybir.AxisListType

B, S, H = 4, 4096, 1024
TOK = 2048
NBLK = 16
NTILE = 32
SLAB = 8
NH, HD, RD = 16, 64, 32
LAT = 256
NE, TOPK = 7, 2
IM = 2048
L1I = 3072
NTIL = 39
NSLOT = NTIL * 128
EPS = 1e-5
TWO_PI_A = 6.28125
TWO_PI_B = float(np.float32(2 * math.pi - 6.28125))
TWO_PI_C = float(2 * math.pi - 6.28125 - np.float64(np.float32(2 * math.pi - 6.28125)))

ENGINES = ("sync", "tensor", "vector", "scalar", "gpsimd")
P0_SUB = 9


class _Handle:
    def __init__(self, op):
        self.op = op

    def then_inc(self, sem, n):
        self.op[3].append((sem, n))
        sem.count += n
        return self


class _Stream:
    def __init__(self):
        self.ops = []

    def wait(self, sem, val=None):
        self.ops.append(("wait", sem, sem.count if val is None else val, None))

    def call(self, method, *args, **kwargs):
        op = ["call", method, (args, kwargs), []]
        self.ops.append(op)
        return _Handle(op)

    def raw(self, fn):
        self.ops.append(("raw", fn, None, None))

    def replay(self, eng):
        for op in self.ops:
            kind = op[0]
            if kind == "wait":
                eng.wait_ge(op[1].h, op[2])
            elif kind == "raw":
                op[1](eng)
            else:
                args, kwargs = op[2]
                try:
                    r = getattr(eng, op[1])(*args, **kwargs)
                except Exception:
                    import sys
                    print("REPLAY FAIL:", op[1],
                          [getattr(a, "shape", a) for a in args], kwargs, file=sys.stderr)
                    raise
                for sem, n in op[3]:
                    r = r.then_inc(sem.h, n)


class Sem:
    def __init__(self, h, name):
        self.h = h
        self.name = name
        self.count = 0


class Prog:
    def __init__(self, nc, stack):
        self.nc = nc
        self.stack = stack
        self.s = {e: _Stream() for e in ENGINES}
        self._semn = 0

    def sem(self, name=None):
        name = name or f"s{self._semn}"
        self._semn += 1
        h = self.stack.enter_context(self.nc.semaphore(name))
        return Sem(h, name)

    def barrier(self, tag=""):
        bs = self.sem(f"bar{self._semn}_{tag}")
        for e in ENGINES:
            self.s[e].call("sem_inc", bs.h, 1)
            bs.count += 1
        for e in ENGINES:
            self.s[e].wait(bs, 5)


def cdiv(a, b):
    return (a + b - 1) // b


DERIVED_SPECS = [
    ("moe_gT", [8, H, IM], F32),
    ("moe_uT", [8, H, IM], F32),
    ("moe_dT", [8, IM, H], F32),
    ("rw_T", [SLAB, 128, NE], F32),
    ("l1_wqT", [H, H], F32),
    ("l1_wkT", [H, H], F32),
    ("l1_wvT", [H, H], F32),
    ("l1_woT", [H, H], F32),
    ("l1_wgT", [H, L1I], I32),
    ("l1_wuT", [H, L1I], I32),
    ("l1_wdT", [L1I, H], I32),
    ("r_bias", [NE, 1], F32),
    ("ln2_col", [128, SLAB], F32),
]

WEIGHT_SPECS = [
    ("L0_attn_A", [16, 16], F32),
    ("L0_attn_B", [16, H], F32),
    ("L0_attn_C", [H, 16], F32),
    ("L0_attn_D", [H, H], F32),
    ("L0_attn_gamma", [1, H], F32),
    ("L0_attn_beta", [1, H], F32),
    ("L0_ffn_ln1_w", [1, H], F32),
    ("L0_ffn_ln2_w", [1, H], F32),
    ("L0_ffn_kv_d", [LAT, H], F32),
    ("L0_ffn_q_d", [LAT, H], F32),
    ("L0_ffn_k_u", [NH * HD, LAT], F32),
    ("L0_ffn_q_u", [NH * HD, LAT], F32),
    ("L0_ffn_v_u", [NH * HD, LAT], F32),
    ("L0_ffn_rope_q_w", [NH * RD, LAT], F32),
    ("L0_ffn_rope_k_w", [NH * RD, H], F32),
    ("L0_ffn_o_w", [H, H], F32),
    ("L0_ffn_router_w", [NE, H], F32),
    ("L0_ffn_routing_bias", [1, NE], F32),
    ("L0_ffn_shared_gate", [IM, H], F32),
    ("L0_ffn_shared_up", [IM, H], F32),
    ("L0_ffn_shared_down", [H, IM], F32),
    ("L0_ffn_routed_gate", [NE, IM, H], F32),
    ("L0_ffn_routed_up", [NE, IM, H], F32),
    ("L0_ffn_routed_down", [NE, H, IM], F32),
    ("L1_attn_wq", [H, H], F32),
    ("L1_attn_wk", [H, H], F32),
    ("L1_attn_wv", [H, H], F32),
    ("L1_attn_wo", [H, H], F32),
    ("L1_ffn_w_gate", [L1I, H], I32),
    ("L1_ffn_w_up", [L1I, H], I32),
    ("L1_ffn_w_down", [H, L1I], I32),
]


class Builder:
    def __init__(self, nc, p, dbg=()):
        self.nc = nc
        self.p = p
        self.dbg = set(dbg)
        self.t = {}
        self.scopes = {}
        self.scope_off = {}
        self.arena = nc.alloc_sbuf_tensor("arena", [128, self.ARENA_COLS], F32).ap()

    ARENA_COLS = 46 * 1024
    SCOPE_REGIONS = {
        "p0x": (0, 16 * 1024),
        "p0": (16 * 1024, 18 * 1024),
        "p1": (18 * 1024, 46 * 1024),
        "p2": (0, 6 * 1024),
        "p2a": (6 * 1024, 26 * 1024),
        "p2b": (26 * 1024, 46 * 1024),
        "p2c": (0, 26 * 1024),
        "p3": (0, 46 * 1024),
        "p4": (0, 46 * 1024),
        "p5": (0, 30 * 1024),
        "p5a": (30 * 1024, 46 * 1024),
    }

    def sb(self, name, shape, dtype=F32, scope=None):
        if scope is None:
            ap = self.nc.alloc_sbuf_tensor(name, shape, dtype).ap()
        else:
            lo, hi = self.SCOPE_REGIONS[scope]
            off = self.scope_off.setdefault(scope, lo)
            esz = {F32: 4, I32: 4, U32: 4, BF16: 2, U8: 1}[dtype]
            ncols = shape[-1]
            fcols = cdiv(ncols * esz, 4)
            fcols = cdiv(fcols, 16) * 16
            assert off + fcols <= hi, (name, scope, off, fcols, hi)
            view = self.arena[0:shape[0], off:off + fcols]
            if dtype is not F32:
                view = view.bitcast(dtype)[:, 0:ncols]
            else:
                view = view[:, 0:ncols]
            self.scope_off[scope] = off + fcols
            ap = view
        self.t[name] = ap
        return ap

    def free_scope(self, scope):
        self.scope_off.pop(scope, None)

    def sep(self, V):
        V.call("tensor_copy", self.junk[:, 0:64], self.junk[:, 64:128])

    def tr(self, T, out_psum, in_sb, pdim=128):
        return T.call("matmul", out_psum, in_sb, self.ident[0:pdim, 0:pdim],
                      is_transpose=True, start=True, stop=True, skip_group_check=True)

    def vround(self, V, dst, src, ti32, tf32):
        small = src.shape[-1] < 64
        V.call("tensor_copy", ti32, src)
        if small:
            self.sep(V)
        V.call("tensor_copy", dst, ti32)
        if small:
            self.sep(V)
        V.call("tensor_tensor", tf32, src, dst, ALU.subtract)
        if small:
            self.sep(V)
        V.call("scalar_tensor_tensor", dst, tf32, 0.5, dst, ALU.is_gt, ALU.add)
        V.call("tensor_scalar", tf32, tf32, -0.5, 0.0, ALU.is_lt, ALU.add)
        if small:
            self.sep(V)
        V.call("tensor_tensor", dst, dst, tf32, ALU.subtract)

    def ps(self, name, shape, dtype=F32):
        ap = self.nc.alloc_psum_tensor(name, shape, dtype).ap()
        self.t[name] = ap
        return ap

    def psum_pool(self):
        self.pA = self.ps("pp_A", [128, 512])
        self.pB = self.ps("pp_B", [128, 512])
        self.pC = self.ps("pp_C", [128, 512])
        self.pD = self.ps("pp_D", [128, 512])
        self.pE = self.ps("pp_E", [128, 1024])

    def dram(self, name, shape, dtype=F32, kind="Internal"):
        if name in self.dbg:
            kind = "ExternalOutput"
        ap = self.nc.dram_tensor(name, shape, dtype, kind=kind).ap()
        self.t[name] = ap
        return ap

    def p0(self):
        p, nc = self.p, self.nc
        V = p.s["vector"]
        T = p.s["tensor"]
        C = p.s["scalar"]
        G = p.s["gpsimd"]
        Y = p.s["sync"]

        self.xseq = self.dram("xseq", [S, H], F32, kind="ExternalInput")
        self.x_own = self.dram("x_own", [TOK, H], F32, kind="ExternalInput")
        self.pos_own = self.dram("pos_own", [1, TOK], F32, kind="ExternalInput")
        self.parity = self.dram("parity", [1, 1], I32, kind="ExternalInput")
        self.own_idx = self.dram("own_idx", [TOK, 1], U32, kind="ExternalInput")
        self.w = {}
        for nm, shp, dt in WEIGHT_SPECS + DERIVED_SPECS:
            self.w[nm] = self.dram("w_" + nm, shp, dt, kind="ExternalInput")
        self.out_own = self.dram("out_own", [TOK, H], F32, kind="ExternalOutput")

        self.Khat = self.dram("Khat", [SLAB, 128, S], F32)
        self.Vhat = self.dram("Vhat", [NTILE, 128, NH * 65], F32)
        self.Qhat = self.dram("Qhat", [SLAB, 128, TOK], F32)
        self.xnownT_d = self.dram("xnownT_d", [SLAB, 128, TOK], F32)
        self.a_tok = self.dram("a_tok", [TOK, H], F32)

        self.ident = self.sb("ident", [128, 128], F32)
        self.iota_p = self.sb("iota_p", [128, 128], F32)
        self.iota_f = self.sb("iota_f", [128, 128], F32)
        self.tri_mask = self.sb("tri_mask", [128, 128], F32)
        self.ones_col = self.sb("ones_col", [1, 128], F32)
        self.par_i = self.sb("par_i", [1, 1], I32)
        self.prot = self.sb("prot", [128, 128], F32)
        self.costab = self.sb("costab", [128, S], F32, scope="p1")
        self.sintab = self.sb("sintab", [128, S], F32, scope="p1")
        self.cos_own = self.sb("cos_own", [128, TOK], F32)
        self.sin_own = self.sb("sin_own", [128, TOK], F32)
        self.ln1w_b = self.sb("ln1w_b", [128, H], F32)
        self.vones = self.sb("vones", [128, 1], F32)
        self.junk = self.sb("junk", [128, 128], F32)

        ld = p.sem("p0ld")
        gp = p.sem("p0gp")
        vk = p.sem("p0vk")
        tk = p.sem("p0tk")
        ck = p.sem("p0ck")

        Y.call("dma_start", out=self.par_i[:], in_=self.parity[:, :]).then_inc(ld, 16)

        G.call("iota", self.iota_p[:], pattern=[[0, 128]], base=0,
               channel_multiplier=1, allow_small_or_imprecise_dtypes=True).then_inc(gp, 1)
        G.call("iota", self.iota_f[:], pattern=[[1, 128]], base=0,
               channel_multiplier=0, allow_small_or_imprecise_dtypes=True).then_inc(gp, 1)

        V.wait(gp)
        V.call("tensor_tensor", self.ident[:], self.iota_p[:], self.iota_f[:], ALU.is_equal)
        V.call("tensor_tensor", self.tri_mask[:], self.iota_p[:], self.iota_f[:], ALU.is_gt)
        V.call("tensor_scalar", self.tri_mask[:], self.tri_mask[:], -1e30, 0.0, ALU.mult, ALU.add)
        V.call("memset", self.ones_col[:], 1.0)
        V.call("memset", self.vones[:], 1.0)
        V.call("memset", self.junk[:], 0.0)
        tmpa = self.sb("p0_tmpa", [128, 128], F32, scope="p0")
        tmpb = self.sb("p0_tmpb", [128, 128], F32, scope="p0")
        V.call("tensor_tensor", tmpa, self.iota_f[:], self.iota_p[:], ALU.subtract)
        V.call("tensor_tensor", tmpb, tmpa, tmpa, ALU.mult)
        V.call("tensor_scalar", tmpb, tmpb, 256.0, 0.0, ALU.is_equal, ALU.add)
        V.call("tensor_tensor", self.prot[:], tmpa, tmpb, ALU.mult)
        pi32 = self.sb("p0_pi32", [128, 128], I32, scope="p0")
        tmpr = self.sb("p0_tmpr", [128, 128], F32, scope="p0")
        tmps = self.sb("p0_tmps", [128, 128], F32, scope="p0")
        V.call("tensor_scalar", tmpr, self.iota_p[:], -15.5, 1.0 / 32.0, ALU.add, ALU.mult)
        self.vround(V, tmpa, tmpr, pi32, tmps)
        V.call("tensor_scalar", tmpr, self.iota_f[:], -15.5, 1.0 / 32.0, ALU.add, ALU.mult)
        self.vround(V, tmpb, tmpr, pi32, tmps)
        V.call("tensor_tensor", tmpa, tmpa, tmpb, ALU.is_equal)
        V.call("tensor_tensor", self.prot[:], self.prot[:], tmpa, ALU.mult)
        V.call("tensor_scalar", self.prot[:], self.prot[:], 1.0 / 16.0, 0.0, ALU.mult, ALU.add)
        V.call("engine_nop").then_inc(vk, 1)

        if P0_SUB < 2:
            self.p0_ld = ld
            self.free_scope("p0")
            self.free_scope("p0x")
            p.barrier("p0")
            return
        invf = self.sb("invf", [128, 1], F32, scope="p0")
        ang = self.sb("ang", [128, S], F32, scope="p0x")
        tmpc = self.sb("p0_tmpc", [128, 1], F32, scope="p0")
        G.call("iota", ang[:, 0:S], pattern=[[1, S]], base=0,
               channel_multiplier=0, allow_small_or_imprecise_dtypes=True).then_inc(gp, 1)
        V.wait(gp)
        pj32 = self.sb("p0_pj32", [128, 1], I32, scope="p0")
        tj1 = self.sb("p0_tj1", [128, 1], F32, scope="p0")
        tj2 = self.sb("p0_tj2", [128, 1], F32, scope="p0")
        tj3 = self.sb("p0_tj3", [128, 1], F32, scope="p0")
        tj4 = self.sb("p0_tj4", [128, 1], F32, scope="p0")
        V.call("tensor_scalar", tj1, self.iota_p[:, 0:1], -7.5, 1.0 / 16.0, ALU.add, ALU.mult)
        self.sep(V)
        self.vround(V, tmpc, tj1, pj32, tj2)
        self.sep(V)
        V.call("tensor_scalar", tj3, tmpc, -16.0, 0.0, ALU.mult, ALU.add)
        self.sep(V)
        V.call("tensor_tensor", tj4, self.iota_p[:, 0:1], tj3, ALU.add)
        self.sep(V)
        V.call("tensor_scalar", invf, tj4, -math.log(10000.0) / 16.0, 0.0, ALU.mult, ALU.add)
        self.sep(V)
        V.call("tensor_copy", self.junk[:, 0:64], self.junk[:, 64:128]).then_inc(vk, 1)
        C.wait(vk, 2)
        C.call("activation", tj3, invf, AF.Exp)
        C.call("activation", self.junk[:, 64:128], self.junk[:, 0:64], AF.Copy).then_inc(ck, 1)
        V.wait(ck)
        V.call("tensor_copy", invf, tj3)
        self.sep(V)
        V.call("tensor_copy", self.junk[:, 0:64], self.junk[:, 64:128]).then_inc(vk, 1)
        C.wait(vk)
        def emit_trig(dst_cos, dst_sin, pos_row_n, ntot):
            for c0 in range(0, ntot, 2048):
                emit_trig_chunk(dst_cos[:, c0:c0 + 2048], dst_sin[:, c0:c0 + 2048],
                                ang[:, c0:c0 + 2048], 2048)

        def emit_trig_chunk(dst_cos, dst_sin, angc, n):
            V.call("tensor_scalar", angc[:, 0:n], angc[:, 0:n], invf, None, ALU.mult)
            kk = self.kk
            V.call("tensor_scalar", self.kku[:, 0:n], angc[:, 0:n], 1.0 / (2 * math.pi), 0.0, ALU.mult, ALU.add)
            self.vround(V, kk[:, 0:n], self.kku[:, 0:n], self.kk32[:, 0:n], self.kkt[:, 0:n])
            V.call("scalar_tensor_tensor", angc[:, 0:n], kk[:, 0:n], -TWO_PI_A, angc[:, 0:n], ALU.mult, ALU.add)
            V.call("scalar_tensor_tensor", angc[:, 0:n], kk[:, 0:n], -TWO_PI_B, angc[:, 0:n], ALU.mult, ALU.add)
            V.call("scalar_tensor_tensor", angc[:, 0:n], kk[:, 0:n], -TWO_PI_C, angc[:, 0:n], ALU.mult, ALU.add)
            V.call("tensor_scalar", angc[:, 0:n], angc[:, 0:n], math.pi, -math.pi, ALU.min, ALU.max).then_inc(vk, 1)
            C.wait(vk)
            C.call("activation", dst_sin[:, 0:n], angc[:, 0:n], AF.Sin).then_inc(ck, 1)
            V.wait(ck)
            V.call("tensor_scalar", angc[:, 0:n], angc[:, 0:n], math.pi / 2, 0.0, ALU.add, ALU.add)
            V.call("tensor_scalar", self.kku[:, 0:n], angc[:, 0:n], 1.0 / (2 * math.pi), 0.0, ALU.mult, ALU.add)
            self.vround(V, kk[:, 0:n], self.kku[:, 0:n], self.kk32[:, 0:n], self.kkt[:, 0:n])
            V.call("scalar_tensor_tensor", angc[:, 0:n], kk[:, 0:n], -TWO_PI_A, angc[:, 0:n], ALU.mult, ALU.add)
            V.call("scalar_tensor_tensor", angc[:, 0:n], kk[:, 0:n], -TWO_PI_B, angc[:, 0:n], ALU.mult, ALU.add)
            V.call("scalar_tensor_tensor", angc[:, 0:n], kk[:, 0:n], -TWO_PI_C, angc[:, 0:n], ALU.mult, ALU.add)
            V.call("tensor_scalar", angc[:, 0:n], angc[:, 0:n], math.pi, -math.pi, ALU.min, ALU.max).then_inc(vk, 1)
            C.wait(vk)
            C.call("activation", dst_cos[:, 0:n], angc[:, 0:n], AF.Sin).then_inc(ck, 1)

        self.kk = self.sb("p0_kk", [128, 2048], F32, scope="p0x")
        self.kku = self.sb("p0_kku", [128, 2048], F32, scope="p0x")
        self.kkt = self.sb("p0_kkt", [128, 2048], F32, scope="p0x")
        self.kk32 = self.sb("p0_kk32", [128, 2048], I32, scope="p0x")
        emit_trig(self.costab, self.sintab, ang[:, 0:S], S)
        posb = self.sb("posb", [1, TOK], F32, scope="p0x")
        Y.call("dma_start", out=posb[:], in_=self.pos_own[:, :]).then_inc(ld, 16)
        T.wait(ld, 32)
        T.wait(vk, 1)
        for ch in range(4):
            T.call("matmul", self.pA[:, :], self.ones_col[:], posb[:, ch * 512:(ch + 1) * 512],
                   start=True, stop=True).then_inc(tk, 1)
            V.wait(tk)
            V.call("tensor_copy", ang[:, ch * 512:(ch + 1) * 512], self.pA[:, :]).then_inc(vk, 1)
            T.wait(vk)
        emit_trig(self.cos_own, self.sin_own, ang[:, 0:TOK], TOK)

        if P0_SUB < 3:
            self.p0_ld = ld
            self.free_scope("p0")
            self.free_scope("p0x")
            p.barrier("p0")
            return
        ln1row = self.sb("ln1row", [1, H], F32, scope="p0")
        Y.call("dma_start", out=ln1row[:], in_=self.w["L0_ffn_ln1_w"][:, :]).then_inc(ld, 16)
        T.wait(ld, None)
        T.call("matmul", self.pE[:, 0:512], self.ones_col[:], ln1row[:, 0:512], start=True, stop=True)
        T.call("matmul", self.pE[:, 512:1024], self.ones_col[:], ln1row[:, 512:1024],
               start=True, stop=True).then_inc(tk, 1)
        V.wait(tk)
        V.call("tensor_copy", self.ln1w_b[:], self.pE[:]).then_inc(vk, 1)
        T.wait(vk)

        if P0_SUB < 4:
            self.p0_ld = ld
            self.free_scope("p0")
            self.free_scope("p0x")
            p.barrier("p0")
            return
        self.kvdT = {s: self.sb(f"kvdT_{s}", [128, SLAB * LAT], BF16, scope="p1") for s in "hl"}
        self.ropekT = {s: self.sb(f"ropekT_{s}", [128, SLAB * 512], BF16, scope="p1") for s in "hl"}
        self.kuT = {s: self.sb(f"kuT_{s}", [128, 2 * 512], BF16, scope="p1") for s in "hl"}
        self.vuT = {s: self.sb(f"vuT_{s}", [128, 2 * H], BF16, scope="p1") for s in "hl"}
        self.BT = {s: self.sb(f"BT_{s}", [128, SLAB * 16], BF16, scope="p1") for s in "hl"}
        wraw = self.sb("p0_wraw", [128, 1024], F32, scope="p0x")
        ptr = self.pA

        def prep_weight(src_ap_rows, n_rowtiles, ncols, dstT, src3d=False):
            n_rows_total = n_rowtiles * 128
            for rt in range(n_rowtiles):
                src = src_ap_rows(rt)
                Y.wait(vk)
                if src3d:
                    for hh in range(4):
                        Y.call("dma_start", out=wraw[hh * 32:(hh + 1) * 32, 0:ncols],
                               in_=src[hh]).then_inc(ld, 16)
                else:
                    Y.call("dma_start", out=wraw[:, 0:ncols], in_=src).then_inc(ld, 16)
                ldt = ld.count
                T.wait(ld, ldt)
                nb = ncols // 128
                for b0 in range(0, nb, 4):
                    for j in range(b0, min(b0 + 4, nb)):
                        h = self.tr(T, ptr[:, (j - b0) * 128:(j - b0 + 1) * 128],
                                    wraw[:, j * 128:(j + 1) * 128])
                    h.then_inc(tk, 1)
                    V.wait(tk)
                    for j in range(b0, min(b0 + 4, nb)):
                        dst_off = j * n_rows_total + rt * 128
                        V.call("tensor_copy", dstT["h"][:, dst_off:dst_off + 128],
                               ptr[:, (j - b0) * 128:(j - b0 + 1) * 128])
                    for j in range(b0, min(b0 + 4, nb)):
                        dst_off = j * n_rows_total + rt * 128
                        V.call("scalar_tensor_tensor", dstT["l"][:, dst_off:dst_off + 128],
                               ptr[:, (j - b0) * 128:(j - b0 + 1) * 128], 1.0,
                               dstT["h"][:, dst_off:dst_off + 128], ALU.mult, ALU.subtract)
                    V.call("engine_nop").then_inc(vk, 1)
                    T.wait(vk)

        kv_d = self.w["L0_ffn_kv_d"]
        prep_weight(lambda rt: kv_d[rt * 128:(rt + 1) * 128, :], 2, H, self.kvdT)
        if P0_SUB < 5:
            self.p0_ld = ld
            self.free_scope("p0")
            self.free_scope("p0x")
            p.barrier("p0")
            return
        ropek = self.w["L0_ffn_rope_k_w"]
        prep_weight(lambda rt: ropek[rt * 128:(rt + 1) * 128, :], 4, H, self.ropekT)
        if P0_SUB < 6:
            self.p0_ld = ld
            self.free_scope("p0")
            self.free_scope("p0x")
            p.barrier("p0")
            return
        k_u = self.w["L0_ffn_k_u"]
        k_u_eff = k_u.rearrange("(h x d) l -> h x d l", h=NH, x=2)[:, 1]
        prep_weight(lambda rt: k_u_eff[rt * 4:(rt + 1) * 4, :, :], 4, LAT, self.kuT, src3d=True)
        if P0_SUB < 7:
            self.p0_ld = ld
            self.free_scope("p0")
            self.free_scope("p0x")
            p.barrier("p0")
            return
        v_u = self.w["L0_ffn_v_u"]
        prep_weight(lambda rt: v_u[rt * 128:(rt + 1) * 128, :], 8, LAT, self.vuT)
        if P0_SUB < 8:
            self.p0_ld = ld
            self.free_scope("p0")
            self.free_scope("p0x")
            p.barrier("p0")
            return
        braw = self.sb("p0_braw", [16, H], F32, scope="p0x")
        Y.call("dma_start", out=braw[:], in_=self.w["L0_attn_B"][:, :]).then_inc(ld, 16)
        ldt = ld.count
        T.wait(ld, ldt)
        for j in range(SLAB):
            h = self.tr(T, ptr[:, (j % 4) * 128:(j % 4) * 128 + 16],
                        braw[:, j * 128:(j + 1) * 128], pdim=16)
            if j % 4 == 3:
                h.then_inc(tk, 1)
                V.wait(tk)
                for jj in range(j - 3, j + 1):
                    off = jj * 16
                    V.call("tensor_copy", self.BT["h"][:, off:off + 16],
                           ptr[:, (jj % 4) * 128:(jj % 4) * 128 + 16])
                for jj in range(j - 3, j + 1):
                    off = jj * 16
                    V.call("scalar_tensor_tensor", self.BT["l"][:, off:off + 16],
                           ptr[:, (jj % 4) * 128:(jj % 4) * 128 + 16], 1.0,
                           self.BT["h"][:, off:off + 16], ALU.mult, ALU.subtract)
                V.call("engine_nop").then_inc(vk, 1)
                T.wait(vk)

        self.p0_ld = ld
        self.free_scope("p0")
        p.barrier("p0")

    def p1(self):
        p = self.p
        V, T, C, G, Y = (p.s[e] for e in ("vector", "tensor", "scalar", "gpsimd", "sync"))

        self.xB_T = self.sb("xB_T", [16, S + 16], F32, scope="p2")
        xt = [self.sb(f"p1_x{i}", [128, H], F32, scope="p1") for i in range(2)]
        xn_tok = self.sb("p1_xn", [128, H], F32, scope="p1")
        ml_tok = self.sb("p1_ml", [128, H], F32, scope="p1")
        stat = self.sb("p1_stat", [128, 16], F32, scope="p1")
        sq = self.sb("p1_sq", [128, H], F32, scope="p1")
        xnT = {s: self.sb(f"p1_xnT_{s}", [128, H], BF16, scope="p1") for s in "hl"}
        mlT = {s: self.sb(f"p1_mlT_{s}", [128, H], BF16, scope="p1") for s in "hl"}
        kvt = {s: self.sb(f"p1_kv_{s}", [128, LAT], BF16, scope="p1") for s in "hl"}
        krp = self.sb("p1_krp", [128, 512], F32, scope="p1")
        Kst = self.sb("p1_Kst", [128, SLAB * 128], F32, scope="p1")
        Vst = self.sb("p1_Vst", [128, NH * 65], F32, scope="p1")

        ptA = self.pA
        ptB = self.pB
        ptC = self.pC
        ptD = self.pD
        ptE = self.pE

        s_x = p.sem("p1_x")
        s_ln = p.sem("p1_ln")
        s_tr = p.sem("p1_tr")
        s_sp = p.sem("p1_sp")
        s_mm = p.sem("p1_mm")
        s_ev = p.sem("p1_ev")
        s_st = p.sem("p1_st")
        s_sc = p.sem("p1_sc")

        V.call("memset", Vst[:], 0.0)
        for h in range(NH):
            V.call("memset", Vst[:, h * 65 + 64:h * 65 + 65], 1.0)
        V.call("memset", self.xB_T[:], 0.0)
        V.call("engine_nop").then_inc(s_ev, 1)

        def emit_ln(xtile, vk_base):
            def half(xin, xout, wmul, s0):
                mu_r, mu, vs, sd, rs0, r2, ee, rs = (stat[:, s0 + i:s0 + i + 1] for i in range(8))
                V.call("tensor_reduce", mu_r, xin, AX.X, ALU.add)
                self.sep(V)
                V.call("tensor_scalar", mu, mu_r, 1.0 / H, 0.0, ALU.mult, ALU.add)
                self.sep(V)
                V.call("tensor_scalar", xout, xin, mu, None, ALU.subtract)
                self.sep(V)
                V.call("tensor_copy", self.junk[:, 0:64], self.junk[:, 64:128]).then_inc(s_ln, 1)
                C.wait(s_ln)
                C.call("activation", sq[:], xout, AF.Square, accum_out=vs)
                C.call("activation", self.junk[:, 64:128], self.junk[:, 0:64], AF.Copy).then_inc(s_sc, 1)
                V.wait(s_sc)
                V.call("tensor_scalar", sd, vs, 1.0 / H, EPS, ALU.mult, ALU.add)
                self.sep(V)
                V.call("tensor_copy", self.junk[:, 0:64], self.junk[:, 64:128]).then_inc(s_ln, 1)
                C.wait(s_ln)
                C.call("activation", rs0, sd, AF.Sqrt)
                C.call("activation", self.junk[:, 64:128], self.junk[:, 0:64], AF.Copy).then_inc(s_sc, 1)
                V.wait(s_sc)
                V.call("reciprocal", mu_r, rs0)
                self.sep(V)
                V.call("tensor_tensor", r2, mu_r, mu_r, ALU.mult)
                self.sep(V)
                V.call("tensor_tensor", ee, r2, sd, ALU.mult)
                self.sep(V)
                V.call("tensor_scalar", r2, ee, -0.5, 1.5, ALU.mult, ALU.add)
                self.sep(V)
                V.call("tensor_tensor", rs, mu_r, r2, ALU.mult)
                self.sep(V)
                V.call("tensor_scalar", xout, xout, rs, None, ALU.mult)
                if wmul is not None:
                    V.call("tensor_tensor", xout, xout, wmul, ALU.mult)

            half(xtile, xn_tok[:], None, 0)
            half(xn_tok[:], ml_tok[:], self.ln1w_b[:], 8)
        if "dbg_p1" in self.dbg:
            d_xn = self.dram("dbg_xn", [128, H], F32)
            d_ml = self.dram("dbg_ml", [128, H], F32)
            d_mlTh = self.dram("dbg_mlTh", [128, H], F32)
            d_kvh = self.dram("dbg_kvh", [128, LAT], F32)
            d_krp = self.dram("dbg_krp", [128, 512], F32)
            d_stat = self.dram("dbg_stat", [128, 16], F32)
            d_xt = self.dram("dbg_xt", [128, H], F32)
            d_bt = self.dram("dbg_bt", [128, 128], F32)
            d_xbp = self.dram("dbg_xbp", [16, 128], F32)
            dmp = self.sb("dmp", [128, H], F32, scope="p1")
        Y.call("dma_start", out=xt[0][:], in_=self.xseq[0:128, :]).then_inc(s_x, 16)

        for i in range(NTILE):
            buf = xt[i % 2]
            if i + 1 < NTILE:
                Y.call("dma_start", out=xt[(i + 1) % 2][:],
                       in_=self.xseq[(i + 1) * 128:(i + 2) * 128, :]).then_inc(s_x, 16)
            V.wait(s_x, 16 * (i + 1))
            emit_ln(buf, None)
            V.call("engine_nop").then_inc(s_ln, 1)
            lnt = s_ln.count

            T.wait(s_ln, lnt)
            for (srct, dstT) in ((xn_tok, xnT), (ml_tok, mlT)):
                for b0 in range(0, SLAB, 4):
                    if s_ev.count > 0:
                        T.wait(s_ev)
                    for j in range(b0, b0 + 4):
                        h = self.tr(T, ptA[:, (j - b0) * 128:(j - b0 + 1) * 128],
                                    srct[:, j * 128:(j + 1) * 128])
                    h.then_inc(s_tr, 1)
                    V.wait(s_tr)
                    for j in range(b0, b0 + 4):
                        off = j * 128
                        V.call("tensor_copy", dstT["h"][:, off:off + 128],
                               ptA[:, (j - b0) * 128:(j - b0 + 1) * 128])
                        V.call("scalar_tensor_tensor", dstT["l"][:, off:off + 128],
                               ptA[:, (j - b0) * 128:(j - b0 + 1) * 128], 1.0,
                               dstT["h"][:, off:off + 128], ALU.mult, ALU.subtract)
                    V.call("engine_nop").then_inc(s_ev, 1)
            spt = s_ev.count

            T.wait(s_ev, spt)
            first = True
            for mch in range(2):
                for ks in range(SLAB):
                    lh = self.kvdT["h"][:, ks * LAT + mch * 128: ks * LAT + (mch + 1) * 128]
                    ll = self.kvdT["l"][:, ks * LAT + mch * 128: ks * LAT + (mch + 1) * 128]
                    rh = mlT["h"][:, ks * 128:(ks + 1) * 128]
                    rl = mlT["l"][:, ks * 128:(ks + 1) * 128]
                    dst = ptB[:, mch * 128:(mch + 1) * 128]
                    T.call("matmul", dst, lh, rh, start=first, stop=False, skip_group_check=True)
                    first = False
                    T.call("matmul", dst, ll, rh, start=False, stop=False, skip_group_check=True)
                    T.call("matmul", dst, lh, rl, start=False, stop=False, skip_group_check=True)
            for ks in range(SLAB):
                lh = self.BT["h"][:, ks * 16:(ks + 1) * 16]
                ll = self.BT["l"][:, ks * 16:(ks + 1) * 16]
                rh = xnT["h"][:, ks * 128:(ks + 1) * 128]
                rl = xnT["l"][:, ks * 128:(ks + 1) * 128]
                dst = ptB[0:16, 256:384]
                last = ks == SLAB - 1
                T.call("matmul", dst, lh, rh, start=False, stop=False, skip_group_check=True)
                T.call("matmul", dst, ll, rh, start=False, stop=False, skip_group_check=True)
                hxb = T.call("matmul", dst, lh, rl, start=False, stop=last, skip_group_check=True)
            hxb.then_inc(s_mm, 1)
            mmt = s_mm.count
            V.wait(s_mm, mmt)
            if i == 0 and "dbg_p1" in self.dbg:
                V.call("tensor_copy", dmp[:, 0:128], self.BT["h"][:, 0:128]).then_inc(s_ev, 1)
                Y.wait(s_ev)
                Y.call("dma_start", out=d_bt[:, :], in_=dmp[:, 0:128]).then_inc(s_st, 16)
                V.wait(s_st)
                V.call("tensor_copy", dmp[0:16, 0:128], ptB[0:16, 256:384]).then_inc(s_ev, 1)
                Y.wait(s_ev)
                Y.call("dma_start", out=d_xbp[:, :], in_=dmp[0:16, 0:128]).then_inc(s_st, 16)
                V.wait(s_st)
            V.call("tensor_copy", kvt["h"][:, :], ptB[:, 0:256])
            V.call("scalar_tensor_tensor", kvt["l"][:, :], ptB[:, 0:256], 1.0, kvt["h"][:, :],
                   ALU.mult, ALU.subtract)
            V.call("tensor_copy", self.xB_T[:, i * 128:(i + 1) * 128], ptB[0:16, 256:384])
            V.call("engine_nop").then_inc(s_ev, 1)
            kvev = s_ev.count

            for q4 in range(4):
                first = True
                for ks in range(SLAB):
                    lh = self.ropekT["h"][:, ks * 512 + q4 * 128: ks * 512 + (q4 + 1) * 128]
                    llo = self.ropekT["l"][:, ks * 512 + q4 * 128: ks * 512 + (q4 + 1) * 128]
                    rh = mlT["h"][:, ks * 128:(ks + 1) * 128]
                    rl = mlT["l"][:, ks * 128:(ks + 1) * 128]
                    dst = ptC[:, q4 * 128:(q4 + 1) * 128]
                    T.call("matmul", dst, lh, rh, start=first, stop=False, skip_group_check=True)
                    first = False
                    T.call("matmul", dst, llo, rh, start=False, stop=False, skip_group_check=True)
                    hrp = T.call("matmul", dst, lh, rl, start=False, stop=(ks == SLAB - 1), skip_group_check=True)
            hrp.then_inc(s_mm, 1)
            T.wait(s_ev, kvev)
            for q4 in range(4):
                first = True
                for ks in range(2):
                    lh = self.kuT["h"][:, ks * 512 + q4 * 128: ks * 512 + (q4 + 1) * 128]
                    llo = self.kuT["l"][:, ks * 512 + q4 * 128: ks * 512 + (q4 + 1) * 128]
                    rh = kvt["h"][:, ks * 128:(ks + 1) * 128]
                    rl = kvt["l"][:, ks * 128:(ks + 1) * 128]
                    dst = ptD[:, q4 * 128:(q4 + 1) * 128]
                    T.call("matmul", dst, lh, rh, start=first, stop=False, skip_group_check=True)
                    first = False
                    T.call("matmul", dst, llo, rh, start=False, stop=False, skip_group_check=True)
                    hku = T.call("matmul", dst, lh, rl, start=False, stop=(ks == 1), skip_group_check=True)
            hku.then_inc(s_mm, 1)
            for nch in range(2):
                first = True
                for ks in range(2):
                    lh = kvt["h"][:, ks * 128:(ks + 1) * 128]
                    llo = kvt["l"][:, ks * 128:(ks + 1) * 128]
                    rh = self.vuT["h"][:, ks * H + nch * 512: ks * H + (nch + 1) * 512]
                    rl = self.vuT["l"][:, ks * H + nch * 512: ks * H + (nch + 1) * 512]
                    dst = ptE[:, nch * 512:(nch + 1) * 512]
                    T.call("matmul", dst, lh, rh, start=first, stop=False, skip_group_check=True)
                    first = False
                    T.call("matmul", dst, llo, rh, start=False, stop=False, skip_group_check=True)
                    hv = T.call("matmul", dst, lh, rl, start=False, stop=(ks == 1), skip_group_check=True)
            hv.then_inc(s_mm, 1)
            mm3t = s_mm.count

            V.wait(s_mm, mm3t - 2)
            V.call("tensor_copy", krp[:], ptC[:]).then_inc(s_ev, 1)
            krev = s_ev.count
            T.wait(s_ev, krev)
            for q4 in range(4):
                T.call("matmul", ptB[:, 384:512], self.prot[:], krp[:, q4 * 128:(q4 + 1) * 128],
                       start=True, stop=True).then_inc(s_mm, 1)
                rott = s_mm.count
                V.wait(s_mm, rott)
                tcol = i * 128
                cs = self.costab[:, tcol:tcol + 128]
                sn = self.sintab[:, tcol:tcol + 128]
                V.call("tensor_tensor", sq[:, 0:128], krp[:, q4 * 128:(q4 + 1) * 128], cs, ALU.mult)
                V.call("tensor_tensor", sq[:, 128:256], ptB[:, 384:512], sn, ALU.mult)
                for piece in range(4):
                    sl = 2 * q4 + piece // 2
                    dst_row = (piece % 2) * 64
                    V.call("tensor_tensor",
                           Kst[dst_row:dst_row + 32, sl * 128:(sl + 1) * 128],
                           sq[piece * 32:piece * 32 + 32, 0:128],
                           sq[piece * 32:piece * 32 + 32, 128:256], ALU.add)
                V.call("engine_nop").then_inc(s_ev, 1)
                T.wait(s_ev)
            V.wait(s_mm, mm3t - 1)
            for q4 in range(4):
                for piece in range(4):
                    sl = 2 * q4 + piece // 2
                    dst_row = (piece % 2) * 64 + 32
                    V.call("tensor_copy",
                           Kst[dst_row:dst_row + 32, sl * 128:(sl + 1) * 128],
                           ptD[piece * 32:piece * 32 + 32, q4 * 128:(q4 + 1) * 128])
            V.wait(s_mm, mm3t)
            V.call("tensor_copy",
                   Vst[:].rearrange("p (h c) -> p h c", h=NH)[:, :, 0:64],
                   ptE[:].rearrange("p (h c) -> p h c", h=NH, c=64))
            V.call("engine_nop").then_inc(s_ev, 1)
            asmt = s_ev.count

            if i == 0 and "dbg_p1" in self.dbg:
                V.call("tensor_copy", dmp[:, 0:16], stat[:]).then_inc(s_ev, 1)
                Y.wait(s_ev)
                Y.call("dma_start", out=d_stat[:, :], in_=dmp[:, 0:16]).then_inc(s_st, 16)
                V.wait(s_st)
                V.call("tensor_copy", dmp[:, 0:H], buf[:]).then_inc(s_ev, 1)
                Y.wait(s_ev)
                Y.call("dma_start", out=d_xt[:, :], in_=dmp[:, 0:H]).then_inc(s_st, 16)
                V.wait(s_st)
                V.call("tensor_copy", dmp[:, 0:H], xn_tok[:])
                Y.wait(s_ev)
                V.call("nop" if False else "engine_nop").then_inc(s_ev, 1)
                Y.wait(s_ev)
                Y.call("dma_start", out=d_xn[:, :], in_=dmp[:, 0:H]).then_inc(s_st, 16)
                V.wait(s_st)
                V.call("tensor_copy", dmp[:, 0:H], ml_tok[:]).then_inc(s_ev, 1)
                Y.wait(s_ev)
                Y.call("dma_start", out=d_ml[:, :], in_=dmp[:, 0:H]).then_inc(s_st, 16)
                V.wait(s_st)
                V.call("tensor_copy", dmp[:, 0:H], mlT["h"][:, 0:H]).then_inc(s_ev, 1)
                Y.wait(s_ev)
                Y.call("dma_start", out=d_mlTh[:, :], in_=dmp[:, 0:H]).then_inc(s_st, 16)
                V.wait(s_st)
                V.call("tensor_copy", dmp[:, 0:LAT], kvt["h"][:, :]).then_inc(s_ev, 1)
                Y.wait(s_ev)
                Y.call("dma_start", out=d_kvh[:, :], in_=dmp[:, 0:LAT]).then_inc(s_st, 16)
                V.wait(s_st)
                V.call("tensor_copy", dmp[:, 0:512], krp[:, :]).then_inc(s_ev, 1)
                Y.wait(s_ev)
                Y.call("dma_start", out=d_krp[:, :], in_=dmp[:, 0:512]).then_inc(s_st, 16)
                V.wait(s_st)
            Y.wait(s_ev, asmt)
            Y.call("dma_start",
                   out=self.Khat[:, :, i * 128:(i + 1) * 128].rearrange("s p c -> p s c"),
                   in_=Kst[:].rearrange("p (s c) -> p s c", s=SLAB)).then_inc(s_st, 16)
            Y.call("dma_start", out=self.Vhat[i, :, :], in_=Vst[:]).then_inc(s_st, 16)
            T.wait(s_st, s_st.count)
            V.wait(s_st, s_st.count)

        self.free_scope("p1")
        p.barrier("p1")


    def p2(self, upto=None):
        p = self.p
        V, T, C, G, Y = (p.s[e] for e in ("vector", "tensor", "scalar", "gpsimd", "sync"))
        ld = p.sem("p2ld")
        gd = p.sem("p2gd")
        tk = p.sem("p2tk")
        vk = p.sem("p2vk")
        sc = p.sem("p2sc")
        st = p.sem("p2st")

        self.qdT = {c: self.sb(f"qdT_{c}", [128, SLAB * LAT], BF16, scope="p2b") for c in "hl"}
        self.quT = {c: self.sb(f"quT_{c}", [128, 2 * 512], BF16, scope="p2b") for c in "hl"}
        self.rqT = {c: self.sb(f"rqT_{c}", [128, 2 * 512], BF16, scope="p2b") for c in "hl"}
        self.DT = {c: self.sb(f"DT_{c}", [128, SLAB * H], BF16, scope="p2b") for c in "hl"}
        self.CTt = self.sb("CTt", [16, SLAB * 128], F32, scope="p2b")
        self.gcol = self.sb("gcol", [128, SLAB], F32, scope="p2b")
        self.bcol = self.sb("bcol", [128, SLAB], F32, scope="p2b")
        wraw = self.sb("p2_wraw", [128, 1024], F32, scope="p2b")
        ptr = self.pA

        def prep(src_rows, n_rowtiles, ncols, dstT, src3d=False):
            for rt in range(n_rowtiles):
                src = src_rows(rt)
                Y.wait(vk)
                if src3d:
                    for hh in range(4):
                        Y.call("dma_start", out=wraw[hh * 32:(hh + 1) * 32, 0:ncols],
                               in_=src[hh]).then_inc(ld, 16)
                else:
                    Y.call("dma_start", out=wraw[:, 0:ncols], in_=src).then_inc(ld, 16)
                T.wait(ld, ld.count)
                nb = ncols // 128
                for b0 in range(0, nb, 4):
                    T.wait(vk)
                    for j in range(b0, min(b0 + 4, nb)):
                        h = self.tr(T, ptr[:, (j - b0) * 128:(j - b0 + 1) * 128],
                                    wraw[:, j * 128:(j + 1) * 128])
                    h.then_inc(tk, 1)
                    V.wait(tk)
                    for j in range(b0, min(b0 + 4, nb)):
                        off = j * n_rowtiles * 128 + rt * 128
                        V.call("tensor_copy", dstT["h"][:, off:off + 128],
                               ptr[:, (j - b0) * 128:(j - b0 + 1) * 128])
                    for j in range(b0, min(b0 + 4, nb)):
                        off = j * n_rowtiles * 128 + rt * 128
                        V.call("scalar_tensor_tensor", dstT["l"][:, off:off + 128],
                               ptr[:, (j - b0) * 128:(j - b0 + 1) * 128], 1.0,
                               dstT["h"][:, off:off + 128], ALU.mult, ALU.subtract)
                    V.call("engine_nop").then_inc(vk, 1)

        q_d = self.w["L0_ffn_q_d"]
        prep(lambda rt: q_d[rt * 128:(rt + 1) * 128, :], 2, H, self.qdT)
        q_u = self.w["L0_ffn_q_u"]
        q_u_eff = q_u.rearrange("(h x d) l -> h x d l", h=NH, x=2)[:, 1]
        prep(lambda rt: q_u_eff[rt * 4:(rt + 1) * 4, :, :], 4, LAT, self.quT, src3d=True)
        rq = self.w["L0_ffn_rope_q_w"]
        prep(lambda rt: rq[rt * 128:(rt + 1) * 128, :], 4, LAT, self.rqT)
        Dw = self.w["L0_attn_D"]
        prep(lambda rt: Dw[rt * 128:(rt + 1) * 128, :], 8, H, self.DT)
        for rt in range(SLAB):
            Y.wait(vk)
            Y.call("dma_start", out=wraw[:, 0:16], in_=self.w["L0_attn_C"][rt * 128:(rt + 1) * 128, :]).then_inc(ld, 16)
            T.wait(ld, ld.count)
            T.wait(vk)
            self.tr(T, ptr[0:16, 0:128], wraw[:, 0:16]).then_inc(tk, 1)
            V.wait(tk)
            V.call("tensor_copy", self.CTt[:, rt * 128:(rt + 1) * 128], ptr[0:16, 0:128])
            V.call("engine_nop").then_inc(vk, 1)
        grow = self.sb("p2_grow", [1, H], F32, scope="p2b")
        for nm, dst in (("L0_attn_gamma", self.gcol), ("L0_attn_beta", self.bcol)):
            Y.wait(vk)
            Y.call("dma_start", out=grow[:], in_=self.w[nm][:, :]).then_inc(ld, 16)
            T.wait(ld, ld.count)
            T.wait(vk)
            for j in range(SLAB):
                hh = self.tr(T, ptr[:, j:j + 1], grow[:, j * 128:(j + 1) * 128], pdim=1)
            hh.then_inc(tk, 1)
            V.wait(tk)
            V.call("tensor_copy", dst[:, :], ptr[:, 0:SLAB])
            V.call("engine_nop").then_inc(vk, 1)
        V.call("tensor_scalar", self.cos_own[:], self.cos_own[:], 0.125, 0.0, ALU.mult, ALU.add)
        V.call("tensor_scalar", self.sin_own[:], self.sin_own[:], 0.125, 0.0, ALU.mult, ALU.add)
        V.call("engine_nop").then_inc(vk, 1)

        if upto == "p2w":
            p.barrier("p2prep")
            return
        An = self.sb("p2_An", [16, 16], F32, scope="p2a")
        Ucur = self.sb("p2_U", [16, 16 * 17], F32, scope="p2a")
        stk1 = self.sb("p2_stk1", [128, 16], F32, scope="p2a")
        stk2 = self.sb("p2_stk2", [128, 16], F32, scope="p2a")
        A16T = self.sb("p2_A16T", [16, 16], F32, scope="p2a")
        A256T = self.sb("p2_A256T", [16, 16], F32, scope="p2a")
        Y.call("dma_start", out=An[:], in_=self.w["L0_attn_A"][:, :]).then_inc(ld, 16)

        REP = self.sb("p2_rep", [128, 3840 + S], F32, scope="p2a")
        REPB = self.sb("p2_repb", [128, 3840 + S], F32, scope="p2a")
        evst = self.sb("p2_evst", [16, 512], F32, scope="p2a")
        stok_st = self.sb("p2_stokst", [128, 64], F32, scope="p2a")
        zpad = self.sb("p2_zpad", [16, 2048], F32, scope="p2a")
        Ybuf = self.dram("p2_Y", [16, 240 + S], F32)
        Zbuf = self.dram("p2_Z", [16, 3840 + S], F32)
        states_tok = self.dram("p2_states_tok", [S, 16], F32)
        sown = self.sb("p2_sown", [16, TOK], F32, scope="p2b")
        self.sown_ap = sown

        def power_family(lhsT_ap, seed_identity, fam):
            V.wait(tk)
            V.wait(ld, ld.count)
            V.call("tensor_copy", Ucur[:, 0:16], self.ident[0:16, 0:16])
            V.call("engine_nop").then_inc(vk, 1)
            for m in range(16):
                T.wait(vk)
                T.call("matmul", self.pB[0:16, 0:16], lhsT_ap, Ucur[:, m * 16:(m + 1) * 16],
                       start=True, stop=True).then_inc(tk, 1)
                V.wait(tk)
                V.call("tensor_copy", Ucur[:, (m + 1) * 16:(m + 2) * 16], self.pB[0:16, 0:16])
                V.call("engine_nop").then_inc(vk, 1)
            Y.wait(vk)
            for m in range(16):
                dststk = stk1 if m < 8 else stk2
                Y.call("dma_start", out=dststk[(m % 8) * 16:(m % 8) * 16 + 16, 0:16],
                       in_=Ucur[:, m * 16:(m + 1) * 16]).then_inc(ld, 16)

        def scan_level3(in_T, in_pad, out_T, sigma, tag, tok_major_out=False):
            W = in_pad + S
            nch = S // 512
            V.wait(ld, ld.count)
            V.wait(gd, gd.count)
            V.wait(tk, tk.count)
            V.call("memset", REP[:, 0:W], 0.0)
            V.call("memset", REPB[:, 0:W], 0.0)
            V.call("engine_nop").then_inc(vk, 1)
            Y.wait(vk)
            for m in range(16):
                shift = m * sigma
                dstR = REP if m < 8 else REPB
                if shift < W:
                    Y.call("dma_start", out=dstR[(m % 8) * 16:(m % 8) * 16 + 16, shift:W],
                           in_=in_T[:, 0:W - shift]).then_inc(ld, 16)
            T.wait(ld, ld.count)
            for c in range(nch):
                T.wait(ld, ld.count)
                T.wait(vk, vk.count)
                T.call("matmul", self.pB[0:16, 0:512], stk1,
                       REP[:, in_pad + c * 512: in_pad + (c + 1) * 512],
                       start=True, stop=False, skip_group_check=True)
                T.call("matmul", self.pB[0:16, 0:512], stk2,
                       REPB[:, in_pad + c * 512: in_pad + (c + 1) * 512],
                       start=False, stop=True, skip_group_check=True).then_inc(tk, 1)
                V.wait(tk)
                V.call("tensor_copy", evst[:, 0:512], self.pB[0:16, 0:512])
                V.call("engine_nop").then_inc(vk, 1)
                if not tok_major_out:
                    Y.wait(vk)
                    Y.call("dma_start", out=out_T[:, c * 512:(c + 1) * 512],
                           in_=evst[:, 0:512]).then_inc(ld, 16)
                else:
                    T.wait(vk, vk.count)
                    for q in range(4):
                        hq = self.tr(T, self.pC[:, q * 16:(q + 1) * 16],
                                     evst[0:16, q * 128:(q + 1) * 128], pdim=16)
                    hq.then_inc(tk, 1)
                    V.wait(tk)
                    V.call("tensor_copy", stok_st[:, 0:64], self.pC[:, 0:64])
                    V.call("engine_nop").then_inc(vk, 1)
                    Y.wait(vk)
                    Y.call("dma_start",
                           out=out_T[c * 512:(c + 1) * 512, :].rearrange("(a p) b -> p a b", a=4),
                           in_=stok_st[:, 0:64].rearrange("p (a b) -> p a b", a=4)).then_inc(ld, 16)
                V.wait(ld, ld.count)

        T.wait(ld, ld.count)
        power_family(An[:, :], None, 0)
        T.wait(ld, ld.count)
        self.tr(T, self.pB[0:16, 0:16], Ucur[:, 256:272], pdim=16).then_inc(tk, 1)
        V.wait(tk)
        V.call("tensor_copy", A16T[:, :], self.pB[0:16, 0:16])
        V.call("engine_nop").then_inc(vk, 1)
        if upto == "p2pow0":
            T.wait(ld, ld.count)
            power_family(An[:, :], None, 0)
            p.barrier("p2prep")
            return
        V.call("memset", zpad[:, :], 0.0)
        V.call("engine_nop").then_inc(vk, 1)
        Y.wait(vk)
        Y.call("dma_start", out=Ybuf[:, 0:240], in_=zpad[:, 0:240]).then_inc(ld, 16)
        for c in range(0, 3840, 1920):
            Y.call("dma_start", out=Zbuf[:, c:c + 1920], in_=zpad[:, 0:1920]).then_inc(ld, 16)
        if upto == "p2zero":
            p.barrier("p2prep")
            return
        scan_level3(self.xB_T[:, 0:S], 0, Ybuf[:, 240:240 + S], 1, "l1")
        if upto == "p2l1":
            p.barrier("p2prep")
            return
        power_family(A16T[:, :], None, 1)
        T.wait(ld, ld.count)
        self.tr(T, self.pB[0:16, 0:16], Ucur[:, 256:272], pdim=16).then_inc(tk, 1)
        V.wait(tk)
        V.call("tensor_copy", A256T[:, :], self.pB[0:16, 0:16])
        V.call("engine_nop").then_inc(vk, 1)
        scan_level3(Ybuf[:, :], 240, Zbuf[:, 3840:3840 + S], 16, "l2")
        power_family(A256T[:, :], None, 2)
        scan_level3(Zbuf[:, :], 3840, states_tok, 256, "l3", tok_major_out=True)

        if upto == "p2scan":
            p.barrier("p2prep")
            return
        oidx = self.sb("p2_oidx", [128, NBLK], U32, scope="p2a")
        stg = self.sb("p2_stg", [128, 16], F32, scope="p2a")
        for j in range(NBLK):
            Y.call("dma_start", out=oidx[:, j:j + 1],
                   in_=self.own_idx[j * 128:(j + 1) * 128, :]).then_inc(ld, 16)
        G.wait(ld, ld.count)
        G.wait(gd, gd.count)
        for j in range(NBLK):
            G.call("dma_start" if False else "indirect_dma_start",
                   out=stg[:, :], out_offset=None,
                   in_=states_tok[:, :],
                   in_offset=IndirectOffsetOnAxis(ap=oidx[:, j:j + 1], axis=0)).then_inc(gd, 16)
            T.wait(gd, gd.count)
            hs = self.tr(T, self.pC[0:16, 0:128], stg[:, 0:16]).then_inc(tk, 1)
            V.wait(tk)
            V.call("tensor_copy", sown[:, j * 128:(j + 1) * 128], self.pC[0:16, 0:128])
            V.call("engine_nop").then_inc(vk, 1)
            G.wait(vk, vk.count)
        self.free_scope("p2")
        p.barrier("p2prep")

    def p2b(self, upto=None):
        p = self.p
        V, T, C, G, Y = (p.s[e] for e in ("vector", "tensor", "scalar", "gpsimd", "sync"))
        ld = p.sem("p2b_ld")
        tk = p.sem("p2b_tk")
        vk = p.sem("p2b_vk")
        sc = p.sem("p2b_sc")
        st = p.sem("p2b_st")

        self.cmax = self.sb("cmax", [128, SLAB], F32)
        spfm = self.sb("p2c_spfm", [128, SLAB * 512], F32, scope="p2c")
        kslab = spfm
        for j in range(SLAB):
            Y.wait(vk)
            Y.call("dma_start", out=kslab[:, :], in_=self.Khat[j, :, :]).then_inc(ld, 16)
            V.wait(ld, ld.count)
            V.call("tensor_reduce", self.cmax[:, j:j + 1], kslab[:, :], AX.X, ALU.max,
                   apply_absolute_value=True)
            V.call("engine_nop").then_inc(vk, 1)

        xt = [self.sb(f"p2c_x{i}", [128, H], F32, scope="p2c") for i in range(2)]
        xn_tok = self.sb("p2c_xn", [128, H], F32, scope="p2c")
        ml_tok = self.sb("p2c_ml", [128, H], F32, scope="p2c")
        stat = self.sb("p2c_stat", [128, 16], F32, scope="p2c")
        sqb = self.sb("p2c_sq", [128, H], F32, scope="p2c")
        xnT = {c: self.sb(f"p2c_xnT_{c}", [128, SLAB * 512], BF16, scope="p2c") for c in "hl"}
        mlg = self.sb("p2c_mlg", [128, 4 * H], F32, scope="p2c")
        afm = self.sb("p2c_afm", [128, SLAB * 512], F32, scope="p2c")
        qlt = {c: self.sb(f"p2b_qlt_{c}", [128, 2 * 512], BF16, scope="p2b") for c in "hl"}
        qrp = self.sb("p2b_qrp", [128, 4 * 512], F32, scope="p2c")
        xn32 = self.sb("p2c_xn32", [128, SLAB * 128], F32, scope="p2c")
        Qst = afm
        rowst = self.sb("p2b_rowst", [1, 16], F32, scope="p2b")
        self.m_bias = self.sb("m_bias", [128, NH * 4], F32)

        if "dbg_p2" in self.dbg:
            d_spfm = self.dram("dbg_spfm", [128, SLAB * 512], F32)
            d_qlt = self.dram("dbg_qlt", [128, 1024], BF16)
            d_sown = self.dram("dbg_sown", [16, TOK], F32)
            d_mlg = self.dram("dbg_mlg", [128, 4 * H], F32)

        if upto == "p2cmax":
            p.barrier("p2b")
            return

        def ln_pair(xtile):
            def half(xin, xout, wmul, s0):
                mu_r, mu, vs, sd, rs0, r2, ee, rs = (stat[:, s0 + i:s0 + i + 1] for i in range(8))
                V.call("tensor_reduce", mu_r, xin, AX.X, ALU.add)
                self.sep(V)
                V.call("tensor_scalar", mu, mu_r, 1.0 / H, 0.0, ALU.mult, ALU.add)
                self.sep(V)
                V.call("tensor_scalar", xout, xin, mu, None, ALU.subtract)
                self.sep(V)
                V.call("tensor_copy", self.junk[:, 0:64], self.junk[:, 64:128]).then_inc(vk, 1)
                C.wait(vk)
                C.call("activation", sqb[:], xout, AF.Square, accum_out=vs)
                C.call("activation", self.junk[:, 64:128], self.junk[:, 0:64], AF.Copy).then_inc(sc, 1)
                V.wait(sc)
                V.call("tensor_scalar", sd, vs, 1.0 / H, EPS, ALU.mult, ALU.add)
                self.sep(V)
                V.call("tensor_copy", self.junk[:, 0:64], self.junk[:, 64:128]).then_inc(vk, 1)
                C.wait(vk)
                C.call("activation", rs0, sd, AF.Sqrt)
                C.call("activation", self.junk[:, 64:128], self.junk[:, 0:64], AF.Copy).then_inc(sc, 1)
                V.wait(sc)
                V.call("reciprocal", mu_r, rs0)
                self.sep(V)
                V.call("tensor_tensor", r2, mu_r, mu_r, ALU.mult)
                self.sep(V)
                V.call("tensor_tensor", ee, r2, sd, ALU.mult)
                self.sep(V)
                V.call("tensor_scalar", r2, ee, -0.5, 1.5, ALU.mult, ALU.add)
                self.sep(V)
                V.call("tensor_tensor", rs, mu_r, r2, ALU.mult)
                self.sep(V)
                V.call("tensor_scalar", xout, xout, rs, None, ALU.mult)
                if wmul is not None:
                    V.call("tensor_tensor", xout, xout, wmul, ALU.mult)
            half(xtile, xn_tok[:], None, 0)
            half(xn_tok[:], ml_tok[:], self.ln1w_b[:], 8)

        def mm3group(dst, lhsT_hl, rhs_hl, kslabs, first, last):
            for i, ks in enumerate(kslabs):
                lh, llo = lhsT_hl(ks)
                rh, rl = rhs_hl(ks)
                T.call("matmul", dst, lh, rh, start=(first and i == 0), stop=False, skip_group_check=True)
                T.call("matmul", dst, llo, rh, start=False, stop=False, skip_group_check=True)
                hm = T.call("matmul", dst, lh, rl, start=False,
                            stop=(last and i == len(kslabs) - 1), skip_group_check=True)
            return hm

        ld_base = ld.count
        lnmarks = {}
        Y.call("dma_start", out=xt[0][:], in_=self.x_own[0:128, :]).then_inc(ld, 16)
        for g in range(4):
            for ti in range(4):
                gi = g * 4 + ti
                if gi + 1 < NBLK:
                    if gi - 1 >= 0:
                        Y.wait(vk, lnmarks[gi - 1])
                    Y.call("dma_start", out=xt[(gi + 1) % 2][:],
                           in_=self.x_own[(gi + 1) * 128:(gi + 2) * 128, :]).then_inc(ld, 16)
                V.wait(ld, ld_base + 16 * (gi + 1))
                ln_pair(xt[gi % 2][:])
                V.call("tensor_copy", mlg[:, ti * H:(ti + 1) * H], ml_tok[:])
                V.call("engine_nop").then_inc(vk, 1)
                lnt = vk.count
                lnmarks[gi] = lnt
                T.wait(vk, lnt)
                for b0 in range(0, SLAB, 4):
                    T.wait(vk)
                    for j in range(b0, b0 + 4):
                        h = self.tr(T, self.pA[:, (j - b0) * 128:(j - b0 + 1) * 128],
                                    xn_tok[:, j * 128:(j + 1) * 128])
                    h.then_inc(tk, 1)
                    V.wait(tk)
                    for j in range(b0, b0 + 4):
                        off = j * 512 + ti * 128
                        V.call("tensor_copy", xnT["h"][:, off:off + 128],
                               self.pA[:, (j - b0) * 128:(j - b0 + 1) * 128])
                    for j in range(b0, b0 + 4):
                        off = j * 512 + ti * 128
                        V.call("scalar_tensor_tensor", xnT["l"][:, off:off + 128],
                               self.pA[:, (j - b0) * 128:(j - b0 + 1) * 128], 1.0,
                               xnT["h"][:, off:off + 128], ALU.mult, ALU.subtract)
                    for j in range(b0, b0 + 4):
                        off = j * 128
                        V.call("tensor_copy", xn32[:, off:off + 128],
                               self.pA[:, (j - b0) * 128:(j - b0 + 1) * 128])
                    V.call("engine_nop").then_inc(vk, 1)
                Y.wait(vk, vk.count)
                Y.call("dma_start",
                       out=self.xnownT_d[:, :, gi * 128:(gi + 1) * 128]
                       .rearrange("s p c -> p s c"),
                       in_=xn32[:, :].rearrange("p (s c) -> p s c", s=SLAB)).then_inc(st, 16)
                V.wait(st, st.count)
            sp_ev = vk.count

            if upto == "p2ln":
                continue
            T.wait(vk, sp_ev)
            for mch in range(SLAB):
                dst = self.pB[:, 0:512]
                mm3group(dst,
                         lambda ks, _m=mch: (self.DT["h"][:, ks * H + _m * 128: ks * H + _m * 128 + 128],
                                             self.DT["l"][:, ks * H + _m * 128: ks * H + _m * 128 + 128]),
                         lambda ks: (xnT["h"][:, ks * 512:(ks + 1) * 512],
                                     xnT["l"][:, ks * 512:(ks + 1) * 512]),
                         list(range(SLAB)), True, False)
                hc = T.call("matmul", dst, self.CTt[:, mch * 128:(mch + 1) * 128],
                            self.sown_ap[0:16, g * 512:(g + 1) * 512],
                            start=False, stop=True, skip_group_check=True).then_inc(tk, 1)
                V.wait(tk)
                V.call("tensor_copy", spfm[:, mch * 512:(mch + 1) * 512], dst)
                V.call("engine_nop").then_inc(vk, 1)
                T.wait(vk)
            if g == 0 and "dbg_p2" in self.dbg:
                Y.wait(vk)
                Y.call("dma_start", out=d_spfm[:, :], in_=spfm[:, 0:SLAB * 512]).then_inc(st, 16)
                Y.call("dma_start", out=d_sown[:, :], in_=self.sown_ap[0:16, 0:TOK]).then_inc(st, 16)
                Y.call("dma_start", out=d_mlg[:, :], in_=mlg[:, :]).then_inc(st, 16)
                V.wait(st, st.count)
            C.wait(vk, vk.count)
            C.call("activation", afm[:, 0:SLAB * 512], spfm[:, 0:SLAB * 512], AF.Square)
            C.call("activation", self.junk[:, 64:128], self.junk[:, 0:64], AF.Copy).then_inc(sc, 1)
            T.wait(sc)
            for ks in range(SLAB):
                T.call("matmul", self.pE[0:1, 0:512], self.vones[:, :],
                       spfm[:, ks * 512:(ks + 1) * 512], start=(ks == 0), stop=(ks == SLAB - 1),
                       skip_group_check=True)
                hq = T.call("matmul", self.pE[0:1, 512:1024], self.vones[:, :],
                            afm[:, ks * 512:(ks + 1) * 512], start=(ks == 0), stop=(ks == SLAB - 1),
                            skip_group_check=True)
            hq.then_inc(tk, 1)
            V.wait(tk)
            r_mu, r_msq, r_var, r_sd, r_rs, r_r2, r_ee, r_rs2 = (stat[0:1, i:i + 1] for i in range(8))
            V.call("tensor_scalar", spfm[0:1, 0:512] if False else afm[0:1, 0:512], self.pE[0:1, 0:512], 1.0 / H, 0.0, ALU.mult, ALU.add)
            V.call("tensor_scalar", afm[0:1, 512:1024], self.pE[0:1, 512:1024], 1.0 / H, 0.0, ALU.mult, ALU.add)
            V.call("tensor_tensor", afm[0:1, 1024:1536], afm[0:1, 0:512], afm[0:1, 0:512], ALU.mult)
            self.sep(V)
            V.call("tensor_tensor", afm[0:1, 1536:2048], afm[0:1, 512:1024], afm[0:1, 1024:1536], ALU.subtract)
            self.sep(V)
            V.call("tensor_scalar", afm[0:1, 1024:1536], afm[0:1, 1536:2048], 1.0, EPS, ALU.mult, ALU.add)
            self.sep(V)
            V.call("tensor_copy", self.junk[:, 0:64], self.junk[:, 64:128]).then_inc(vk, 1)
            C.wait(vk)
            C.call("activation", afm[0:1, 1536:2048], afm[0:1, 1024:1536], AF.Sqrt)
            C.call("activation", self.junk[:, 64:128], self.junk[:, 0:64], AF.Copy).then_inc(sc, 1)
            V.wait(sc)
            V.call("reciprocal", afm[0:1, 2048:2560], afm[0:1, 1536:2048])
            self.sep(V)
            V.call("tensor_tensor", afm[0:1, 2560:3072], afm[0:1, 2048:2560], afm[0:1, 2048:2560], ALU.mult)
            self.sep(V)
            V.call("tensor_tensor", afm[0:1, 3072:3584], afm[0:1, 2560:3072], afm[0:1, 1024:1536], ALU.mult)
            self.sep(V)
            V.call("tensor_scalar", afm[0:1, 2560:3072], afm[0:1, 3072:3584], -0.5, 1.5, ALU.mult, ALU.add)
            self.sep(V)
            V.call("tensor_tensor", afm[0:1, 3072:3584], afm[0:1, 2048:2560], afm[0:1, 2560:3072], ALU.mult)
            self.sep(V)
            V.call("tensor_copy", self.junk[:, 0:64], self.junk[:, 64:128]).then_inc(vk, 1)
            T.wait(vk)
            T.call("matmul", self.pE[:, 0:512], self.ones_col[:, :], afm[0:1, 0:512],
                   start=True, stop=True, skip_group_check=True)
            hb = T.call("matmul", self.pE[:, 512:1024], self.ones_col[:, :], afm[0:1, 3072:3584],
                        start=True, stop=True, skip_group_check=True).then_inc(tk, 1)
            V.wait(tk)
            for ks in range(SLAB):
                sl = slice(ks * 512, (ks + 1) * 512)
                V.call("tensor_tensor", afm[:, sl], spfm[:, sl], self.pE[:, 0:512], ALU.subtract)
                V.call("tensor_tensor", afm[:, sl], afm[:, sl], self.pE[:, 512:1024], ALU.mult)
                V.call("tensor_scalar", afm[:, sl], afm[:, sl], self.gcol[:, ks:ks + 1],
                       self.bcol[:, ks:ks + 1], ALU.mult, ALU.add)
            V.call("engine_nop").then_inc(vk, 1)
            aev = vk.count
            T.wait(vk, aev)
            for ti in range(4):
                for b0 in range(0, SLAB, 4):
                    T.wait(vk)
                    for j in range(b0, b0 + 4):
                        h = self.tr(T, self.pA[:, (j - b0) * 128:(j - b0 + 1) * 128],
                                    afm[:, j * 512 + ti * 128: j * 512 + (ti + 1) * 128])
                    h.then_inc(tk, 1)
                    V.wait(tk)
                    for j in range(b0, b0 + 4):
                        V.call("tensor_copy", sqb[:, j * 128:(j + 1) * 128],
                               self.pA[:, (j - b0) * 128:(j - b0 + 1) * 128])
                    V.call("engine_nop").then_inc(vk, 1)
                Y.wait(vk)
                Y.call("dma_start", out=self.a_tok[(g * 4 + ti) * 128:(g * 4 + ti + 1) * 128, :],
                       in_=sqb[:, 0:H]).then_inc(st, 16)
                V.wait(st, st.count)

            if upto == "p2ssm":
                continue
            for ti in range(4):
                for b0 in range(0, SLAB, 4):
                    T.wait(vk)
                    for j in range(b0, b0 + 4):
                        h = self.tr(T, self.pA[:, (j - b0) * 128:(j - b0 + 1) * 128],
                                    mlg[:, ti * H + j * 128: ti * H + (j + 1) * 128])
                    h.then_inc(tk, 1)
                    V.wait(tk)
                    for j in range(b0, b0 + 4):
                        off = j * 512 + ti * 128
                        V.call("tensor_copy", xnT["h"][:, off:off + 128],
                               self.pA[:, (j - b0) * 128:(j - b0 + 1) * 128])
                    for j in range(b0, b0 + 4):
                        off = j * 512 + ti * 128
                        V.call("scalar_tensor_tensor", xnT["l"][:, off:off + 128],
                               self.pA[:, (j - b0) * 128:(j - b0 + 1) * 128], 1.0,
                               xnT["h"][:, off:off + 128], ALU.mult, ALU.subtract)
                    V.call("engine_nop").then_inc(vk, 1)
            mlev = vk.count
            T.wait(vk, mlev)
            for mch in range(2):
                dst = self.pB[:, 0:512]
                hm = mm3group(dst,
                              lambda ks, _m=mch: (self.qdT["h"][:, ks * LAT + _m * 128: ks * LAT + (_m + 1) * 128],
                                                  self.qdT["l"][:, ks * LAT + _m * 128: ks * LAT + (_m + 1) * 128]),
                              lambda ks: (xnT["h"][:, ks * 512:(ks + 1) * 512],
                                          xnT["l"][:, ks * 512:(ks + 1) * 512]),
                              list(range(SLAB)), True, True)
                hm.then_inc(tk, 1)
                V.wait(tk)
                V.call("tensor_copy", qlt["h"][:, mch * 512:(mch + 1) * 512], dst)
                V.call("scalar_tensor_tensor", qlt["l"][:, mch * 512:(mch + 1) * 512], dst, 1.0,
                       qlt["h"][:, mch * 512:(mch + 1) * 512], ALU.mult, ALU.subtract)
                V.call("engine_nop").then_inc(vk, 1)
                T.wait(vk)
            if g == 0 and "dbg_p2" in self.dbg:
                Y.wait(vk)
                Y.call("dma_start", out=d_qlt[:, :], in_=qlt["h"][:, 0:1024]).then_inc(st, 16)
                V.wait(st, st.count)
            for q4 in range(4):
                dst = self.pC[:, 0:512]
                hm = mm3group(dst,
                              lambda ks, _q=q4: (self.rqT["h"][:, ks * 512 + _q * 128: ks * 512 + (_q + 1) * 128],
                                                 self.rqT["l"][:, ks * 512 + _q * 128: ks * 512 + (_q + 1) * 128]),
                              lambda ks: (qlt["h"][:, ks * 512:(ks + 1) * 512],
                                          qlt["l"][:, ks * 512:(ks + 1) * 512]),
                              [0, 1], True, True).then_inc(tk, 1)
                V.wait(tk)
                V.call("tensor_copy", qrp[:, q4 * 512:(q4 + 1) * 512], dst)
                V.call("engine_nop").then_inc(vk, 1)
                T.wait(vk)
                dst = self.pD[:, 0:512]
                hm = mm3group(dst,
                              lambda ks, _q=q4: (self.quT["h"][:, ks * 512 + _q * 128: ks * 512 + (_q + 1) * 128],
                                                 self.quT["l"][:, ks * 512 + _q * 128: ks * 512 + (_q + 1) * 128]),
                              lambda ks: (qlt["h"][:, ks * 512:(ks + 1) * 512],
                                          qlt["l"][:, ks * 512:(ks + 1) * 512]),
                              [0, 1], True, True).then_inc(tk, 1)
                V.wait(tk)
                for piece in range(4):
                    sl2 = 2 * q4 + piece // 2
                    dst_row = (piece % 2) * 64 + 32
                    V.call("tensor_scalar",
                           Qst[dst_row:dst_row + 32, sl2 * 512:(sl2 + 1) * 512],
                           dst[piece * 32:piece * 32 + 32, 0:512], 0.125, 0.0, ALU.mult, ALU.add)
                V.call("engine_nop").then_inc(vk, 1)
                T.wait(vk)
                T.call("matmul", self.pD[:, 0:512], self.prot[:], qrp[:, q4 * 512:(q4 + 1) * 512],
                       start=True, stop=True).then_inc(tk, 1)
                V.wait(tk)
                cs = self.cos_own[:, g * 512:(g + 1) * 512]
                sn = self.sin_own[:, g * 512:(g + 1) * 512]
                V.call("tensor_tensor", sqb[:, 0:512], qrp[:, q4 * 512:(q4 + 1) * 512], cs, ALU.mult)
                V.call("tensor_tensor", sqb[:, 512:1024], self.pD[:, 0:512], sn, ALU.mult)
                for piece in range(4):
                    sl2 = 2 * q4 + piece // 2
                    dst_row = (piece % 2) * 64
                    V.call("tensor_tensor",
                           Qst[dst_row:dst_row + 32, sl2 * 512:(sl2 + 1) * 512],
                           sqb[piece * 32:piece * 32 + 32, 0:512],
                           sqb[piece * 32:piece * 32 + 32, 512:1024], ALU.add)
                V.call("engine_nop").then_inc(vk, 1)
                T.wait(vk)
            C.wait(vk, vk.count)
            C.call("activation", spfm[:, 0:SLAB * 512], Qst[:, 0:SLAB * 512], AF.Abs)
            C.call("activation", self.junk[:, 64:128], self.junk[:, 0:64], AF.Copy).then_inc(sc, 1)
            T.wait(sc)
            for hh in range(NH):
                sl2 = hh // 2
                strip = (hh % 2) * 64
                hb = T.call("matmul", self.pE[0:1, 0:512],
                            self.cmax[strip:strip + 64, sl2:sl2 + 1],
                            spfm[strip:strip + 64, sl2 * 512:(sl2 + 1) * 512],
                            start=True, stop=True, skip_group_check=True,
                            tile_position=(strip, 0)).then_inc(tk, 1)
                V.wait(tk)
                V.call("tensor_reduce", rowst[0:1, 0:1], self.pE[0:1, 0:512], AX.X, ALU.max)
                self.sep(V)
                V.call("tensor_scalar", rowst[0:1, 1:2], rowst[0:1, 0:1], -1.01, -2.0, ALU.mult, ALU.add)
                self.sep(V)
                V.call("tensor_copy", self.junk[:, 0:64], self.junk[:, 64:128]).then_inc(vk, 1)
                T.wait(vk)
                hb2 = T.call("matmul", self.pE[0:64, 512:513], self.ones_col[:, 0:64], rowst[0:1, 1:2],
                             start=True, stop=True, skip_group_check=True)
                hb3 = T.call("matmul", self.pE[64:128, 512:513], self.ones_col[:, 64:128], rowst[0:1, 1:2],
                             start=True, stop=True, skip_group_check=True, tile_position=(0, 64)).then_inc(tk, 1)
                V.wait(tk)
                V.call("tensor_copy", self.m_bias[:, hh * 4 + g: hh * 4 + g + 1], self.pE[:, 512:513])
                V.call("engine_nop").then_inc(vk, 1)
                T.wait(vk)
            Y.wait(vk, vk.count)
            Y.call("dma_start",
                   out=self.Qhat[:, :, g * 512:(g + 1) * 512].rearrange("s p c -> p s c"),
                   in_=Qst[:].rearrange("p (s c) -> p s c", s=SLAB)).then_inc(st, 16)
            T.wait(st, st.count)
            V.wait(st, st.count)

        if upto in ("p2ln", "p2ssm"):
            pass
        self.free_scope("p2a")
        self.free_scope("p2b")
        self.free_scope("p2c")
        p.barrier("p2b")

    def p3(self):
        p = self.p
        V, T, C, G, Y = (p.s[e] for e in ("vector", "tensor", "scalar", "gpsimd", "sync"))
        ld = p.sem("p3ld")
        tk = p.sem("p3tk")
        vk = p.sem("p3vk")
        sc = p.sem("p3sc")
        st = p.sem("p3st")

        self.mla_out = self.dram("mla_out", [SLAB, 128, TOK], F32)

        maskA01 = self.sb("p3_maskA", [128, 128], F32, scope="p3")
        maskB01 = self.sb("p3_maskB", [128, 128], F32, scope="p3")
        tri01 = self.sb("p3_tri01", [128, 128], F32, scope="p3")
        parf = self.sb("p3_parf", [1, 1], F32, scope="p3")
        parcol = self.sb("p3_parcol", [128, 1], F32, scope="p3")
        pc1m = self.sb("p3_pc1m", [128, 1], F32, scope="p3")
        V.call("tensor_copy", parf, self.par_i[0:1, 0:1])
        V.call("tensor_tensor", tri01, self.iota_p[:], self.iota_f[:], ALU.is_le)
        V.call("engine_nop").then_inc(vk, 1)
        T.wait(vk)
        T.call("matmul", self.pA[:, 0:1], self.ones_col[:], parf,
               start=True, stop=True, skip_group_check=True).then_inc(tk, 1)
        V.wait(tk)
        V.call("tensor_copy", parcol, self.pA[:, 0:1])
        self.sep(V)
        V.call("tensor_scalar", pc1m, parcol, -1.0, 1.0, ALU.mult, ALU.add)
        self.sep(V)
        V.call("tensor_scalar", maskA01, tri01, pc1m, 0.0, ALU.mult, ALU.add)
        V.call("tensor_scalar", maskB01, tri01, parcol, 0.0, ALU.mult, ALU.add)
        self.sep(V)
        V.call("tensor_scalar", maskA01, maskA01, parcol, 0.0, ALU.add, ALU.add)
        V.call("engine_nop").then_inc(vk, 1)

        self.owT = {c: self.sb(f"p3_owT_{c}", [128, SLAB * H], BF16, scope="p3") for c in "hl"}
        wraw = self.sb("p3_wraw", [128, H], F32, scope="p3")
        for rt in range(SLAB):
            Y.wait(vk, vk.count)
            Y.call("dma_start", out=wraw[:, :],
                   in_=self.w["L0_ffn_o_w"][rt * 128:(rt + 1) * 128, :]).then_inc(ld, 16)
            T.wait(ld, ld.count)
            for b0 in range(0, SLAB, 4):
                T.wait(vk, vk.count)
                for j in range(b0, b0 + 4):
                    h = self.tr(T, self.pA[:, (j - b0) * 128:(j - b0 + 1) * 128],
                                wraw[:, j * 128:(j + 1) * 128])
                h.then_inc(tk, 1)
                V.wait(tk)
                for j in range(b0, b0 + 4):
                    off = j * H + rt * 128
                    V.call("tensor_copy", self.owT["h"][:, off:off + 128],
                           self.pA[:, (j - b0) * 128:(j - b0 + 1) * 128])
                for j in range(b0, b0 + 4):
                    off = j * H + rt * 128
                    V.call("scalar_tensor_tensor", self.owT["l"][:, off:off + 128],
                           self.pA[:, (j - b0) * 128:(j - b0 + 1) * 128], 1.0,
                           self.owT["h"][:, off:off + 128], ALU.mult, ALU.subtract)
                V.call("engine_nop").then_inc(vk, 1)

        Kb = self.sb("p3_Kb", [128, S], F32, scope="p3")
        Vb = self.sb("p3_Vb", [128, NTILE * 130], F32, scope="p3")
        Qb = self.sb("p3_Qb", [128, TOK], F32, scope="p3")
        PT = self.sb("p3_PT", [128, 2 * S], F32, scope="p3")
        ctxh = self.sb("p3_ctxh", [128, SLAB * TOK], BF16, scope="p3")
        ctxl = self.sb("p3_ctxl", [128, SLAB * TOK], BF16, scope="p3")
        stage = self.sb("p3_stage", [128, 128], F32, scope="p3")
        srow = self.sb("p3_srow", [1, 256], F32, scope="p3")
        r0 = self.sb("p3_r0", [1, 256], F32, scope="p3")
        re_ = self.sb("p3_re", [1, 256], F32, scope="p3")
        rr = self.sb("p3_rr", [1, 256], F32, scope="p3")

        for hp in range(SLAB):
            Y.wait(vk, vk.count)
            Y.wait(tk, tk.count)
            Y.call("dma_start", out=Kb[:, :], in_=self.Khat[hp, :, :]).then_inc(ld, 16)
            Y.call("dma_start", out=Qb[:, :], in_=self.Qhat[hp, :, :]).then_inc(ld, 16)
            Y.call("dma_start", out=Vb[:, :].rearrange("p (t c) -> p t c", t=NTILE),
                   in_=self.Vhat[:, :, 2 * hp * 65:(2 * hp + 2) * 65]
                   .rearrange("t p c -> p t c")).then_inc(ld, 16)
            T.wait(ld, ld.count)
            for qt in range(NBLK):
                nkt = 2 * qt + 2
                mcol = (2 * hp) * 4 + qt // 4
                mcolB = (2 * hp + 1) * 4 + qt // 4
                for g0 in range(0, nkt, 8):
                    gsz = min(8, nkt - g0)
                    if g0 > 0:
                        T.wait(sc)
                    for i in range(gsz):
                        kt = g0 + i
                        T.call("matmul", self.pE[:, i * 128:(i + 1) * 128],
                               Kb[0:64, kt * 128:(kt + 1) * 128],
                               Qb[0:64, qt * 128:(qt + 1) * 128],
                               start=True, stop=True, skip_group_check=True,
                               tile_position=(0, 0))
                        dstB = self.pA if i < 4 else self.pB
                        hsb = T.call("matmul", dstB[:, (i % 4) * 128:(i % 4 + 1) * 128],
                                     Kb[64:128, kt * 128:(kt + 1) * 128],
                                     Qb[64:128, qt * 128:(qt + 1) * 128],
                                     start=True, stop=True, skip_group_check=True,
                                     tile_position=(64, 0))
                    hsb.then_inc(tk, 1)
                    C.wait(tk)
                    na = min(gsz, 4)
                    C.call("activation", PT[:, g0 * 128:(g0 + na) * 128],
                           self.pE[:, 0:na * 128], AF.Exp,
                           bias=self.m_bias[:, mcol:mcol + 1], scale=1.0)
                    if gsz > 4:
                        C.call("activation", PT[:, (g0 + 4) * 128:(g0 + gsz) * 128],
                               self.pE[:, 512:gsz * 128], AF.Exp,
                               bias=self.m_bias[:, mcol:mcol + 1], scale=1.0)
                    C.call("activation", PT[:, S + g0 * 128:S + (g0 + na) * 128],
                           self.pA[:, 0:na * 128], AF.Exp,
                           bias=self.m_bias[:, mcolB:mcolB + 1], scale=1.0)
                    if gsz > 4:
                        C.call("activation", PT[:, S + (g0 + 4) * 128:S + (g0 + gsz) * 128],
                               self.pB[:, 0:(gsz - 4) * 128], AF.Exp,
                               bias=self.m_bias[:, mcolB:mcolB + 1], scale=1.0)
                    C.call("activation", self.junk[:, 64:128], self.junk[:, 0:64],
                           AF.Copy).then_inc(sc, 1)
                V.wait(sc)
                for kt, mk in ((nkt - 2, maskA01), (nkt - 1, maskB01)):
                    V.call("tensor_tensor", PT[:, kt * 128:(kt + 1) * 128],
                           PT[:, kt * 128:(kt + 1) * 128], mk, ALU.mult)
                    V.call("tensor_tensor", PT[:, S + kt * 128:S + (kt + 1) * 128],
                           PT[:, S + kt * 128:S + (kt + 1) * 128], mk, ALU.mult)
                V.call("tensor_copy", self.junk[:, 0:64],
                       self.junk[:, 64:128]).then_inc(vk, 1)
                T.wait(vk, vk.count)
                for kt in range(nkt):
                    T.call("matmul", self.pC[0:65, 0:128],
                           Vb[:, kt * 130:kt * 130 + 65],
                           PT[:, kt * 128:(kt + 1) * 128],
                           start=(kt == 0), stop=(kt == nkt - 1), skip_group_check=True)
                for kt in range(nkt):
                    hpv = T.call("matmul", self.pC[0:65, 128:256],
                                 Vb[:, kt * 130 + 65:kt * 130 + 130],
                                 PT[:, S + kt * 128:S + (kt + 1) * 128],
                                 start=(kt == 0), stop=(kt == nkt - 1),
                                 skip_group_check=True)
                hpv.then_inc(tk, 1)
                V.wait(tk)
                V.call("tensor_copy", srow[0:1, :], self.pC[64:65, 0:256])
                V.call("reciprocal", r0[0:1, :], srow[0:1, :])
                V.call("tensor_tensor", re_[0:1, :], srow[0:1, :], r0[0:1, :], ALU.mult)
                V.call("tensor_scalar", re_[0:1, :], re_[0:1, :], -1.0, 2.0, ALU.mult, ALU.add)
                V.call("tensor_tensor", rr[0:1, :], r0[0:1, :], re_[0:1, :], ALU.mult)
                V.call("engine_nop").then_inc(vk, 1)
                T.wait(vk)
                T.call("matmul", self.pD[0:64, 0:128], self.ones_col[0:1, 0:64],
                       rr[0:1, 0:128], start=True, stop=True, skip_group_check=True)
                hbc = T.call("matmul", self.pD[64:128, 0:128], self.ones_col[0:1, 64:128],
                             rr[0:1, 128:256], start=True, stop=True,
                             skip_group_check=True, tile_position=(0, 64)).then_inc(tk, 1)
                V.wait(tk)
                col = hp * TOK + qt * 128
                V.call("tensor_copy", stage[0:64, :], self.pC[0:64, 0:128])
                V.call("tensor_copy", stage[64:128, :], self.pC[0:64, 128:256])
                V.call("tensor_tensor", stage[:, :], stage[:, :],
                       self.pD[:, 0:128], ALU.mult)
                V.call("tensor_copy", ctxh[:, col:col + 128], stage[:, :])
                V.call("scalar_tensor_tensor", ctxl[:, col:col + 128], stage[:, :], 1.0,
                       ctxh[:, col:col + 128], ALU.mult, ALU.subtract)
                V.call("engine_nop").then_inc(vk, 1)

        ost = self.sb("p3_ost", [128, 512], F32, scope="p3")
        T.wait(vk, vk.count)
        for mch in range(SLAB):
            for cc in range(TOK // 512):
                first = True
                for ks in range(SLAB):
                    lh = self.owT["h"][:, ks * H + mch * 128: ks * H + (mch + 1) * 128]
                    llo = self.owT["l"][:, ks * H + mch * 128: ks * H + (mch + 1) * 128]
                    rh = ctxh[:, ks * TOK + cc * 512: ks * TOK + (cc + 1) * 512]
                    rl = ctxl[:, ks * TOK + cc * 512: ks * TOK + (cc + 1) * 512]
                    T.call("matmul", self.pB[:, 0:512], lh, rh, start=first, stop=False,
                           skip_group_check=True)
                    first = False
                    T.call("matmul", self.pB[:, 0:512], llo, rh, start=False, stop=False,
                           skip_group_check=True)
                    ho = T.call("matmul", self.pB[:, 0:512], lh, rl, start=False,
                                stop=(ks == SLAB - 1), skip_group_check=True)
                ho.then_inc(tk, 1)
                V.wait(tk)
                V.call("tensor_copy", ost[:, :], self.pB[:, 0:512])
                V.call("engine_nop").then_inc(vk, 1)
                Y.wait(vk)
                Y.call("dma_start", out=self.mla_out[mch, :, cc * 512:(cc + 1) * 512],
                       in_=ost[:, :]).then_inc(st, 16)
                T.wait(st, st.count)
                V.wait(st, st.count)
        self.free_scope("p3")
        p.barrier("p3")

    def p4(self):
        p = self.p
        V, T, C, G, Y = (p.s[e] for e in ("vector", "tensor", "scalar", "gpsimd", "sync"))
        ld = p.sem("p4ld")
        tk = p.sem("p4tk")
        vk = p.sem("p4vk")
        sc = p.sem("p4sc")
        st = p.sem("p4st")

        self.t2_d = self.dram("t2_d", [SLAB, 128, TOK], F32)
        HF = 1024

        acc = self.sb("p4_acc", [128, SLAB * HF], F32, scope="p4")
        xg = self.sb("p4_xg", [128, SLAB * HF], F32, scope="p4")
        Hst = self.sb("p4_H", [128, 16 * 512], F32, scope="p4")
        dTh = self.sb("p4_dTh", [128, 16 * 512], F32, scope="p4")
        graw = self.sb("p4_gr", [128, SLAB * 128], F32, scope="p4")
        uraw = self.sb("p4_ur", [128, SLAB * 128], F32, scope="p4")
        Hg = self.sb("p4_Hg", [128, 512], F32, scope="p4")
        tmpe = self.sb("p4_te", [128, 512], F32, scope="p4")
        sqt = self.sb("p4_sq", [128, 512], F32, scope="p4")
        lgt = self.sb("p4_lg", [NE, HF], F32, scope="p4")
        prb = self.sb("p4_pr", [NE, HF], F32, scope="p4")
        ptt = self.sb("p4_ptt", [128, 8 * NE], F32, scope="p4")
        OH1 = self.sb("p4_oh1", [128, 8 * NE], F32, scope="p4")
        OH2 = self.sb("p4_oh2", [128, 8 * NE], F32, scope="p4")
        eA = self.sb("p4_eA", [128, 8 * NE], F32, scope="p4")
        wgtt = self.sb("p4_wgtt", [128, 8 * NE], F32, scope="p4")
        m1 = self.sb("p4_m1", [128, 8], F32, scope="p4")
        m2 = self.sb("p4_m2", [128, 8], F32, scope="p4")
        i1 = self.sb("p4_i1", [128, 8], F32, scope="p4")
        i2 = self.sb("p4_i2", [128, 8], F32, scope="p4")
        den = self.sb("p4_den", [128, 8], F32, scope="p4")
        rde = self.sb("p4_rde", [128, 8], F32, scope="p4")
        w1c = self.sb("p4_w1c", [128, 8], F32, scope="p4")
        w2c = self.sb("p4_w2c", [128, 8], F32, scope="p4")
        rws = self.sb("p4_rws", [1, 2048], F32, scope="p4")
        wgtb = self.sb("p4_wgtb", [128, HF], F32, scope="p4")
        diag = self.sb("p4_diag", [128, 128], F32, scope="p4")
        obuf = self.sb("p4_ones", [128, 128], F32, scope="p4")
        rwT = self.sb("p4_rwT", [128, SLAB * NE], F32, scope="p4")
        rb = self.sb("p4_rb", [NE, 1], F32, scope="p4")
        ln2c = self.sb("p4_ln2c", [128, SLAB], F32, scope="p4")

        Y.call("dma_start", out=rwT[:, :].rearrange("p (s c) -> p s c", s=SLAB),
               in_=self.w["rw_T"][:, :, :].rearrange("s p c -> p s c")).then_inc(ld, 16)
        Y.call("dma_start", out=rb[:, :], in_=self.w["r_bias"][:, :]).then_inc(ld, 16)
        Y.call("dma_start", out=ln2c[:, :], in_=self.w["ln2_col"][:, :]).then_inc(ld, 16)
        V.wait(ld, ld.count)
        V.call("memset", obuf[:], 1.0)
        V.call("engine_nop").then_inc(vk, 1)
        i7 = self.iota_f[:, 0:NE]

        for hf in range(2):
            T0 = hf * HF
            Y.wait(vk, vk.count)
            Y.wait(tk, tk.count)
            for ks in range(SLAB):
                Y.call("dma_start", out=acc[:, ks * HF:(ks + 1) * HF],
                       in_=self.xnownT_d[ks, :, T0:T0 + HF]).then_inc(ld, 16)
                Y.call("dma_start", out=xg[:, ks * HF:(ks + 1) * HF],
                       in_=self.mla_out[ks, :, T0:T0 + HF]).then_inc(ld, 16)
            V.wait(ld, ld.count)
            for ks in range(SLAB):
                V.call("tensor_tensor", acc[:, ks * HF:(ks + 1) * HF],
                       acc[:, ks * HF:(ks + 1) * HF], xg[:, ks * HF:(ks + 1) * HF], ALU.add)
            V.call("engine_nop").then_inc(vk, 1)
            C.wait(vk, vk.count)
            for tc in range(2):
                hcol = tc * 512
                for ks in range(SLAB):
                    C.wait(tk, tk.count)
                    C.call("activation", sqt[:, :], acc[:, ks * HF + hcol:ks * HF + hcol + 512],
                           AF.Square)
                    C.call("activation", self.junk[:, 64:128], self.junk[:, 0:64],
                           AF.Copy).then_inc(sc, 1)
                    T.wait(sc)
                    T.call("matmul", self.pC[0:1, 0:512], self.vones[:, :],
                           acc[:, ks * HF + hcol:ks * HF + hcol + 512],
                           start=(ks == 0), stop=(ks == SLAB - 1), skip_group_check=True)
                    T.call("matmul", self.pD[0:1, 0:512], self.vones[:, :], sqt[:, :],
                           start=(ks == 0), stop=(ks == SLAB - 1),
                           skip_group_check=True).then_inc(tk, 1)
                V.wait(tk, tk.count)
                mu_r, rs_r, t1r, t2r = (rws[0:1, i * 512:(i + 1) * 512] for i in range(4))
                V.call("tensor_scalar", mu_r, self.pC[0:1, 0:512], 1.0 / H, 0.0, ALU.mult, ALU.add)
                V.call("tensor_scalar", rs_r, self.pD[0:1, 0:512], 1.0 / H, 0.0, ALU.mult, ALU.add)
                self.sep(V)
                V.call("tensor_tensor", t1r, mu_r, mu_r, ALU.mult)
                self.sep(V)
                V.call("tensor_tensor", t2r, rs_r, t1r, ALU.subtract)
                self.sep(V)
                V.call("tensor_scalar", t1r, t2r, 1.0, EPS, ALU.mult, ALU.add)
                self.sep(V)
                V.call("tensor_copy", self.junk[:, 0:64], self.junk[:, 64:128]).then_inc(vk, 1)
                C.wait(vk)
                C.call("activation", t2r, t1r, AF.Sqrt)
                C.call("activation", self.junk[:, 64:128], self.junk[:, 0:64],
                       AF.Copy).then_inc(sc, 1)
                V.wait(sc)
                V.call("reciprocal", rs_r, t2r)
                self.sep(V)
                V.call("tensor_tensor", t2r, t2r, rs_r, ALU.mult)
                self.sep(V)
                V.call("tensor_scalar", t2r, t2r, -1.0, 2.0, ALU.mult, ALU.add)
                self.sep(V)
                V.call("tensor_tensor", rs_r, rs_r, t2r, ALU.mult)
                self.sep(V)
                V.call("tensor_copy", self.junk[:, 0:64], self.junk[:, 64:128]).then_inc(vk, 1)
                T.wait(vk)
                T.call("matmul", self.pE[:, 0:512], self.ones_col[:], mu_r,
                       start=True, stop=True, skip_group_check=True)
                T.call("matmul", self.pE[:, 512:1024], self.ones_col[:], rs_r,
                       start=True, stop=True, skip_group_check=True).then_inc(tk, 1)
                V.wait(tk)
                for ks in range(SLAB):
                    xsl = xg[:, ks * HF + hcol:ks * HF + hcol + 512]
                    V.call("tensor_tensor", xsl, acc[:, ks * HF + hcol:ks * HF + hcol + 512],
                           self.pE[:, 0:512], ALU.subtract)
                    V.call("tensor_tensor", xsl, xsl, self.pE[:, 512:1024], ALU.mult)
                    V.call("tensor_scalar", xsl, xsl, ln2c[:, ks:ks + 1], None, ALU.mult)
                V.call("engine_nop").then_inc(vk, 1)
                T.wait(vk)
                dstP = self.pA if tc == 0 else self.pB
                for ks in range(SLAB):
                    hl = T.call("matmul", dstP[0:NE, 0:512], rwT[:, ks * NE:(ks + 1) * NE],
                                xg[:, ks * HF + hcol:ks * HF + hcol + 512],
                                start=(ks == 0), stop=(ks == SLAB - 1), skip_group_check=True)
                hl.then_inc(tk, 1)
            V.wait(tk, tk.count)
            V.call("tensor_copy", lgt[:, 0:512], self.pA[0:NE, 0:512])
            V.call("tensor_copy", lgt[:, 512:1024], self.pB[0:NE, 0:512])
            self.sep(V)
            V.call("tensor_scalar", lgt[:, :], lgt[:, :], rb[:, 0:1], 0.0, ALU.add, ALU.add)
            self.sep(V)
            V.call("tensor_copy", self.junk[:, 0:64], self.junk[:, 64:128]).then_inc(vk, 1)
            C.wait(vk)
            C.call("activation", prb[:, :], lgt[:, :], AF.Sigmoid)
            C.call("activation", self.junk[:, 64:128], self.junk[:, 0:64],
                   AF.Copy).then_inc(sc, 1)
            T.wait(sc)
            for b0 in range(0, 8, 4):
                for q in range(4):
                    h = self.tr(T, self.pA[:, q * 128:q * 128 + NE],
                                prb[0:NE, (b0 + q) * 128:(b0 + q + 1) * 128], pdim=NE)
                h.then_inc(tk, 1)
                V.wait(tk)
                for q in range(4):
                    V.call("tensor_copy", ptt[:, (b0 + q) * NE:(b0 + q + 1) * NE],
                           self.pA[:, q * 128:q * 128 + NE])
                V.call("engine_nop").then_inc(vk, 1)
                T.wait(vk)
            kc = lambda k: slice(k * NE, (k + 1) * NE)
            for k in range(8):
                V.call("tensor_reduce", m1[:, k:k + 1], ptt[:, kc(k)], AX.X, ALU.max)
            self.sep(V)
            for k in range(8):
                V.call("tensor_scalar", eA[:, kc(k)], ptt[:, kc(k)], m1[:, k:k + 1], None,
                       ALU.is_equal)
            self.sep(V)
            for k in range(8):
                V.call("tensor_scalar", eA[:, kc(k)], eA[:, kc(k)], -1e4, 1e4, ALU.mult, ALU.add)
            self.sep(V)
            for k in range(8):
                V.call("tensor_tensor", eA[:, kc(k)], eA[:, kc(k)], i7, ALU.add)
            self.sep(V)
            for k in range(8):
                V.call("tensor_reduce", i1[:, k:k + 1], eA[:, kc(k)], AX.X, ALU.min)
            self.sep(V)
            for k in range(8):
                V.call("tensor_scalar", OH1[:, kc(k)], i7, i1[:, k:k + 1], None, ALU.is_equal)
            self.sep(V)
            for k in range(8):
                V.call("scalar_tensor_tensor", eA[:, kc(k)], OH1[:, kc(k)], -4.0, ptt[:, kc(k)],
                       ALU.mult, ALU.add)
            self.sep(V)
            for k in range(8):
                V.call("tensor_reduce", m2[:, k:k + 1], eA[:, kc(k)], AX.X, ALU.max)
            self.sep(V)
            for k in range(8):
                V.call("tensor_scalar", eA[:, kc(k)], eA[:, kc(k)], m2[:, k:k + 1], None,
                       ALU.is_equal)
            self.sep(V)
            for k in range(8):
                V.call("tensor_scalar", eA[:, kc(k)], eA[:, kc(k)], -1e4, 1e4, ALU.mult, ALU.add)
            self.sep(V)
            for k in range(8):
                V.call("tensor_tensor", eA[:, kc(k)], eA[:, kc(k)], i7, ALU.add)
            self.sep(V)
            for k in range(8):
                V.call("tensor_reduce", i2[:, k:k + 1], eA[:, kc(k)], AX.X, ALU.min)
            self.sep(V)
            for k in range(8):
                V.call("tensor_scalar", OH2[:, kc(k)], i7, i2[:, k:k + 1], None, ALU.is_equal)
            self.sep(V)
            V.call("tensor_tensor", den[:, :], m1[:, :], m2[:, :], ALU.add)
            self.sep(V)
            V.call("reciprocal", rde[:, :], den[:, :])
            self.sep(V)
            V.call("tensor_tensor", w1c[:, :], den[:, :], rde[:, :], ALU.mult)
            self.sep(V)
            V.call("tensor_scalar", w1c[:, :], w1c[:, :], -1.0, 2.0, ALU.mult, ALU.add)
            self.sep(V)
            V.call("tensor_tensor", rde[:, :], rde[:, :], w1c[:, :], ALU.mult)
            self.sep(V)
            V.call("tensor_tensor", w1c[:, :], m1[:, :], rde[:, :], ALU.mult)
            V.call("tensor_tensor", w2c[:, :], m2[:, :], rde[:, :], ALU.mult)
            self.sep(V)
            for k in range(8):
                V.call("tensor_scalar", wgtt[:, kc(k)], OH1[:, kc(k)], w1c[:, k:k + 1], None,
                       ALU.mult)
            self.sep(V)
            for k in range(8):
                V.call("tensor_scalar", eA[:, kc(k)], OH2[:, kc(k)], w2c[:, k:k + 1], None,
                       ALU.mult)
            self.sep(V)
            V.call("tensor_tensor", wgtt[:, :], wgtt[:, :], eA[:, :], ALU.add)
            V.call("engine_nop").then_inc(vk, 1)

            for e in range(8):
                if e < NE:
                    T.wait(vk, vk.count)
                    for tl in range(8):
                        V.wait(tk, tk.count)
                        V.call("tensor_scalar", diag[:, :], self.ident[:, :],
                               wgtt[:, tl * NE + e:tl * NE + e + 1], None, ALU.mult)
                        V.call("engine_nop").then_inc(vk, 1)
                        T.wait(vk)
                        T.call("matmul", self.pD[:, (tl % 4) * 128:(tl % 4 + 1) * 128],
                               obuf[:, :], diag[:, :], start=True, stop=True,
                               skip_group_check=True).then_inc(tk, 1)
                        if tl % 4 == 3:
                            V.wait(tk, tk.count)
                            V.call("tensor_copy", wgtb[:, (tl // 4) * 512:(tl // 4 + 1) * 512],
                                   self.pD[:, 0:512])
                            V.call("engine_nop").then_inc(vk, 1)
                            T.wait(vk, vk.count)
                for tc in range(2):
                    hcol = tc * 512
                    for mch in range(16):
                        Y.wait(tk, tk.count)
                        Y.wait(vk, vk.count)
                        Y.call("dma_start",
                               out=graw[:, :].rearrange("p (s c) -> p s c", s=SLAB),
                               in_=self.w["moe_gT"][e, :, mch * 128:(mch + 1) * 128]
                               .rearrange("(s p) c -> p s c", p=128)).then_inc(ld, 16)
                        Y.call("dma_start",
                               out=uraw[:, :].rearrange("p (s c) -> p s c", s=SLAB),
                               in_=self.w["moe_uT"][e, :, mch * 128:(mch + 1) * 128]
                               .rearrange("(s p) c -> p s c", p=128)).then_inc(ld, 16)
                        T.wait(ld, ld.count)
                        for ks in range(SLAB):
                            T.call("matmul", self.pE[:, 0:512], graw[:, ks * 128:(ks + 1) * 128],
                                   xg[:, ks * HF + hcol:ks * HF + hcol + 512],
                                   start=(ks == 0), stop=(ks == SLAB - 1), skip_group_check=True)
                            hu = T.call("matmul", self.pE[:, 512:1024],
                                        uraw[:, ks * 128:(ks + 1) * 128],
                                        xg[:, ks * HF + hcol:ks * HF + hcol + 512],
                                        start=(ks == 0), stop=(ks == SLAB - 1),
                                        skip_group_check=True)
                        hu.then_inc(tk, 1)
                        C.wait(tk)
                        C.call("activation", Hg[:, :], self.pE[:, 0:512], AF.Silu)
                        C.call("activation", self.junk[:, 64:128], self.junk[:, 0:64],
                               AF.Copy).then_inc(sc, 1)
                        V.wait(sc)
                        V.call("tensor_tensor", Hst[:, mch * 512:(mch + 1) * 512], Hg[:, :],
                               self.pE[:, 512:1024], ALU.mult)
                        V.call("engine_nop").then_inc(vk, 1)
                        T.wait(vk)
                    for hh in range(2):
                        Y.wait(tk, tk.count)
                        for ks in range(16):
                            Y.call("dma_start", out=dTh[:, ks * 512:(ks + 1) * 512],
                                   in_=self.w["moe_dT"][e, ks * 128:(ks + 1) * 128,
                                                        hh * 512:(hh + 1) * 512]).then_inc(ld, 16)
                        T.wait(ld, ld.count)
                        for hp2 in range(2):
                            for ks in range(16):
                                T.call("matmul", self.pA[:, 0:512],
                                       dTh[:, ks * 512 + 2 * hp2 * 128:ks * 512 + (2 * hp2 + 1) * 128],
                                       Hst[:, ks * 512:(ks + 1) * 512],
                                       start=(ks == 0), stop=(ks == 15), skip_group_check=True)
                                hd = T.call("matmul", self.pB[:, 0:512],
                                            dTh[:, ks * 512 + (2 * hp2 + 1) * 128:ks * 512 + (2 * hp2 + 2) * 128],
                                            Hst[:, ks * 512:(ks + 1) * 512],
                                            start=(ks == 0), stop=(ks == 15),
                                            skip_group_check=True)
                            hd.then_inc(tk, 1)
                            V.wait(tk)
                            for which, pX in ((0, self.pA), (1, self.pB)):
                                hs = hh * 4 + 2 * hp2 + which
                                asl = acc[:, hs * HF + hcol:hs * HF + hcol + 512]
                                if e < NE:
                                    V.call("tensor_tensor", tmpe[:, :], pX[:, 0:512],
                                           wgtb[:, hcol:hcol + 512], ALU.mult)
                                    V.call("tensor_tensor", asl, asl, tmpe[:, :], ALU.add)
                                else:
                                    V.call("tensor_tensor", asl, asl, pX[:, 0:512], ALU.add)
                            V.call("engine_nop").then_inc(vk, 1)
                            T.wait(vk)
            Y.wait(vk, vk.count)
            for ks in range(SLAB):
                Y.call("dma_start", out=self.t2_d[ks, :, T0:T0 + HF],
                       in_=acc[:, ks * HF:(ks + 1) * HF]).then_inc(st, 16)
            T.wait(st, st.count)
            V.wait(st, st.count)
        self.free_scope("p4")
        p.barrier("p4")

    def p5(self):
        p = self.p
        V, T, C, G, Y = (p.s[e] for e in ("vector", "tensor", "scalar", "gpsimd", "sync"))
        ld = p.sem("p5ld")
        tk = p.sem("p5tk")
        vk = p.sem("p5vk")
        sc = p.sem("p5sc")
        st = p.sem("p5st")
        cc = p.sem("p5cc")
        gd = p.sem("p5gd")
        S127 = float(np.float32(1.0 / 127.0))

        self.x1_d = self.dram("x1_d", [TOK, H], F32)
        self.xn1_d = self.dram("xn1_d", [TOK, H], F32)
        ctx_own = self.nc.dram_tensor("ctx_own", [TOK, H], F32, kind="Internal").ap()
        if "ctx_dbg" in self.dbg:
            self.ctx_dbg = self.dram("ctx_dbg", [TOK, H], F32)
            self.qkv_dbg = self.dram("qkv_dbg", [3, TOK, H], F32)
            self.bq_dbg = self.dram("bq_dbg", [TOK, H], F32)
            self.hf_dbg = self.dram("hf_dbg", [128, 24 * 256], F32)
            self.wd_dbg = self.dram("wd_dbg", [3, 128, SLAB * H], F32)
            self.dpe_dbg = self.dram("dpe_dbg", [128, H], F32)
            self.gu_dbg = self.dram("gu_dbg", [2, 128, 512], F32)
            self.sx_dbg = self.dram("sx_dbg", [TOK, 1], F32)
            self.wq_dbg = self.dram("wq_dbg", [128, SLAB * H], F32)
        ctx_pair = self.nc.dram_tensor("ctx_pair", [2 * TOK, H], F32, kind="Internal").ap()
        self.ctx_d = self.dram("ctx_d", [S, H], F32)
        self.a1_d = self.dram("a1_d", [TOK, H], F32)

        wq_i = self.sb("p5_wq", [128, SLAB * H], F32, scope="p5")
        wk_i = self.sb("p5_wk", [128, SLAB * H], F32, scope="p5")
        wv_i = self.sb("p5_wv", [128, SLAB * H], F32, scope="p5")
        tmpb = self.sb("p5_tmp", [128, H], F32, scope="p5")
        ti32 = self.sb("p5_ti", [128, H], I32, scope="p5")
        tf32 = self.sb("p5_tf", [128, H], F32, scope="p5")
        rowsum = self.sb("p5_rows", [128, 1], F32, scope="p5")
        scal = self.sb("p5_scal", [1, 16], F32, scope="p5")
        swcol = self.sb("p5_swcol", [128, 4], F32, scope="p5")
        swmul = self.sb("p5_swmul", [128, 4], F32, scope="p5")
        xtile = self.sb("p5_xt", [128, H], F32, scope="p5a")
        atile = self.sb("p5_at", [128, H], F32, scope="p5a")
        x1t = self.sb("p5_x1", [128, H], F32, scope="p5a")
        xn1t = self.sb("p5_xn1", [128, H], F32, scope="p5a")
        rqt = self.sb("p5_rq", [128, H], F32, scope="p5")
        xqt = self.sb("p5_xq", [128, H], F32, scope="p5")
        xqT = self.sb("p5_xqT", [128, SLAB * 128], F32, scope="p5a")
        stat = self.sb("p5_stat", [128, 16], F32, scope="p5")
        qb = self.sb("p5_qb", [128, H], F32, scope="p5a")
        kb = self.sb("p5_kb", [128, H], F32, scope="p5a")
        vb = self.sb("p5_vb", [128, H], F32, scope="p5a")
        sh = self.sb("p5_sh", [128, 256], F32, scope="p5a")
        prodb = self.sb("p5_prod", [128, H], F32, scope="p5a")
        rmax = self.sb("p5_rmax", [128, 16], F32, scope="p5a")
        rsum = self.sb("p5_rsum", [128, 16], F32, scope="p5a")
        ctxt = self.sb("p5_ctx", [128, H], F32, scope="p5a")

        def bcast_col(dst_col, src11, hb_sem):
            V.call("tensor_copy", self.junk[:, 0:64], self.junk[:, 64:128]).then_inc(vk, 1)
            T.wait(vk, vk.count)
            h = T.call("matmul", self.pD[:, 0:1], self.ones_col[:], src11,
                       start=True, stop=True, skip_group_check=True).then_inc(hb_sem, 1)
            V.wait(hb_sem)
            V.call("tensor_copy", dst_col, self.pD[:, 0:1])
            self.sep(V)

        def recip_newton(dst, src, t1, t2):
            V.call("reciprocal", dst, src)
            self.sep(V)
            V.call("tensor_tensor", t1, src, dst, ALU.mult)
            self.sep(V)
            V.call("tensor_scalar", t2, t1, -1.0, 2.0, ALU.mult, ALU.add)
            self.sep(V)
            V.call("tensor_tensor", dst, dst, t2, ALU.mult)
            self.sep(V)

        def quant_weight(dst, wname, widx):
            Y.wait(vk, vk.count)
            Y.call("dma_start", out=dst[:, :].rearrange("p (s c) -> p s c", s=SLAB),
                   in_=self.w[wname][:, :].rearrange("(s p) c -> p s c", p=128)
                   ).then_inc(ld, 16)
            C.wait(ld, ld.count)
            for kchunk in range(SLAB):
                C.call("activation", tmpb[:, :], dst[:, kchunk * H:(kchunk + 1) * H],
                       AF.Abs, accum_out=stat[:, kchunk:kchunk + 1])
            C.call("activation", self.junk[:, 64:128], self.junk[:, 0:64],
                   AF.Copy).then_inc(sc, 1)
            V.wait(sc)
            V.call("tensor_reduce", rowsum[:, 0:1], stat[:, 0:SLAB], AX.X, ALU.add)
            self.sep(V)
            V.call("tensor_copy", self.junk[:, 0:64], self.junk[:, 64:128]).then_inc(vk, 1)
            T.wait(vk)
            T.call("matmul", self.pD[0:1, 0:1], rowsum[:, 0:1], self.vones[:, :],
                   start=True, stop=True, skip_group_check=True).then_inc(tk, 1)
            V.wait(tk)
            V.call("tensor_scalar", scal[0:1, 0:1], self.pD[0:1, 0:1],
                   1.0 / (H * H), 0.0, ALU.mult, ALU.add)
            self.sep(V)
            V.call("tensor_scalar", scal[0:1, 1:2], scal[0:1, 0:1], EPS, 0.0,
                   ALU.max, ALU.add)
            self.sep(V)
            recip_newton(scal[0:1, 2:3], scal[0:1, 1:2], scal[0:1, 3:4], scal[0:1, 4:5])
            V.call("tensor_copy", self.junk[:, 0:64], self.junk[:, 64:128]).then_inc(vk, 1)
            bcast_col(swmul[:, widx:widx + 1], scal[0:1, 2:3], tk)
            bcast_col(swcol[:, widx:widx + 1], scal[0:1, 1:2], tk)
            for kchunk in range(SLAB):
                dch = dst[:, kchunk * H:(kchunk + 1) * H]
                V.call("tensor_scalar", tmpb[:, :], dch, swmul[:, widx:widx + 1], None,
                       ALU.mult)
                self.vround(V, dch, tmpb[:, :], ti32[:, :], tf32[:, :])
                V.call("tensor_scalar", dch, dch, 1.0, 0.0, ALU.min, ALU.add)
                V.call("tensor_scalar", dch, dch, -1.0, 0.0, ALU.max, ALU.add)
            V.call("engine_nop").then_inc(vk, 1)

        quant_weight(wq_i, "l1_wqT", 0)
        quant_weight(wk_i, "l1_wkT", 1)
        quant_weight(wv_i, "l1_wvT", 2)

        sh3 = sh[:, :].rearrange("p (o h) -> p h o", o=16)

        def rms_quant(src, sxi):
            C.wait(vk, vk.count)
            C.call("activation", tmpb[:, :], src, AF.Square, accum_out=stat[:, 8:9])
            C.call("activation", self.junk[:, 64:128], self.junk[:, 0:64],
                   AF.Copy).then_inc(sc, 1)
            V.wait(sc)
            V.call("tensor_scalar", stat[:, 9:10], stat[:, 8:9], 1.0 / H, 1e-6,
                   ALU.mult, ALU.add)
            self.sep(V)
            V.call("tensor_copy", self.junk[:, 0:64], self.junk[:, 64:128]).then_inc(vk, 1)
            C.wait(vk)
            C.call("activation", stat[:, 10:11], stat[:, 9:10], AF.Sqrt)
            C.call("activation", self.junk[:, 64:128], self.junk[:, 0:64],
                   AF.Copy).then_inc(sc, 1)
            V.wait(sc)
            recip_newton(stat[:, 11:12], stat[:, 10:11], stat[:, 12:13], stat[:, 13:14])
            V.call("tensor_scalar", rqt[:, :], src, stat[:, 11:12], None, ALU.mult)
            V.call("tensor_reduce", stat[:, 12:13], rqt[:, :], AX.X, ALU.max,
                   apply_absolute_value=True)
            self.sep(V)
            V.call("tensor_scalar", stat[:, 12:13], stat[:, 12:13], EPS, 0.0,
                   ALU.max, ALU.add)
            self.sep(V)
            recip_newton(stat[:, 13:14], stat[:, 12:13], stat[:, 14:15], stat[:, 15:16])
            V.call("tensor_scalar", stat[:, 13:14], stat[:, 13:14], 127.0, 0.0,
                   ALU.mult, ALU.add)
            self.sep(V)
            V.call("tensor_scalar", tmpb[:, :], rqt[:, :], stat[:, 13:14], None, ALU.mult)
            self.vround(V, xqt[:, :], tmpb[:, :], ti32[:, :], tf32[:, :])
            recip_newton(stat[:, sxi:sxi + 1], stat[:, 13:14],
                         stat[:, 14:15], stat[:, 15:16])

        def qkv_tile(dsts, widxs, wbufs):
            V.call("tensor_copy", self.junk[:, 0:64], self.junk[:, 64:128]).then_inc(vk, 1)
            T.wait(vk, vk.count)
            for b0 in range(0, SLAB, 4):
                T.wait(vk, vk.count)
                for j in range(b0, b0 + 4):
                    h = self.tr(T, self.pA[:, (j - b0) * 128:(j - b0 + 1) * 128],
                                xqt[:, j * 128:(j + 1) * 128])
                h.then_inc(tk, 1)
                V.wait(tk)
                for j in range(b0, b0 + 4):
                    V.call("tensor_copy", xqT[:, j * 128:(j + 1) * 128],
                           self.pA[:, (j - b0) * 128:(j - b0 + 1) * 128])
                V.call("engine_nop").then_inc(vk, 1)
            T.wait(vk, vk.count)
            for wi, (dstb, widx, wb) in enumerate(zip(dsts, widxs, wbufs)):
                for nh in range(2):
                    dstP = (self.pE[:, 0:512], self.pE[:, 512:1024],
                            self.pA[:, 0:512], self.pB[:, 0:512],
                            self.pC[:, 0:512], self.pD[:, 0:512])[wi * 2 + nh]
                    for ks in range(SLAB):
                        hm = T.call("matmul", dstP, xqT[:, ks * 128:(ks + 1) * 128],
                                    wb[:, ks * H + nh * 512:ks * H + (nh + 1) * 512],
                                    start=(ks == 0), stop=(ks == SLAB - 1),
                                    skip_group_check=True)
                hm.then_inc(tk, 1)
            V.wait(tk)
            for wi, (dstb, widx, wb) in enumerate(zip(dsts, widxs, wbufs)):
                V.call("tensor_tensor", stat[:, 8:9], stat[:, 0:1],
                       swcol[:, widx:widx + 1], ALU.mult)
                self.sep(V)
                for nh in range(2):
                    dstP = (self.pE[:, 0:512], self.pE[:, 512:1024],
                            self.pA[:, 0:512], self.pB[:, 0:512],
                            self.pC[:, 0:512], self.pD[:, 0:512])[wi * 2 + nh]
                    V.call("tensor_scalar", dstb[:, nh * 512:(nh + 1) * 512], dstP,
                           stat[:, 8:9], None, ALU.mult)
            V.call("engine_nop").then_inc(vk, 1)
            T.wait(vk, vk.count)

        for ti in range(NBLK):
            r0 = ti * 128
            Y.wait(vk, vk.count)
            Y.wait(st, st.count)
            for ks in range(SLAB):
                Y.call("dma_start", out=tmpb[:, ks * 128:(ks + 1) * 128],
                       in_=self.t2_d[ks, :, r0:r0 + 128]).then_inc(ld, 16)
            Y.call("dma_start", out=xtile[:, :],
                   in_=self.x_own[r0:r0 + 128, :]).then_inc(ld, 16)
            Y.call("dma_start", out=atile[:, :],
                   in_=self.a_tok[r0:r0 + 128, :]).then_inc(ld, 16)
            T.wait(ld, ld.count)
            for b0 in range(0, SLAB, 4):
                T.wait(vk, vk.count)
                for j in range(b0, b0 + 4):
                    h = self.tr(T, self.pA[:, (j - b0) * 128:(j - b0 + 1) * 128],
                                tmpb[:, j * 128:(j + 1) * 128])
                h.then_inc(tk, 1)
                V.wait(tk)
                for j in range(b0, b0 + 4):
                    V.call("tensor_copy", x1t[:, j * 128:(j + 1) * 128],
                           self.pA[:, (j - b0) * 128:(j - b0 + 1) * 128])
                V.call("engine_nop").then_inc(vk, 1)
            V.call("tensor_tensor", x1t[:, :], x1t[:, :], xtile[:, :], ALU.add)
            V.call("tensor_tensor", x1t[:, :], x1t[:, :], atile[:, :], ALU.add)
            V.call("engine_nop").then_inc(vk, 1)
            Y.wait(vk, vk.count)
            Y.call("dma_start", out=self.x1_d[r0:r0 + 128, :], in_=x1t[:, :]).then_inc(st, 16)
            mu_r, mu, vs, sd, rs0, r2, ee, rs = (stat[:, i:i + 1] for i in range(8))
            V.call("tensor_reduce", mu_r, x1t[:, :], AX.X, ALU.add)
            self.sep(V)
            V.call("tensor_scalar", mu, mu_r, 1.0 / H, 0.0, ALU.mult, ALU.add)
            self.sep(V)
            V.call("tensor_scalar", xn1t[:, :], x1t[:, :], mu, None, ALU.subtract)
            self.sep(V)
            V.call("tensor_copy", self.junk[:, 0:64], self.junk[:, 64:128]).then_inc(vk, 1)
            C.wait(vk)
            C.call("activation", tmpb[:, :], xn1t[:, :], AF.Square, accum_out=vs)
            C.call("activation", self.junk[:, 64:128], self.junk[:, 0:64],
                   AF.Copy).then_inc(sc, 1)
            V.wait(sc)
            V.call("tensor_scalar", sd, vs, 1.0 / H, EPS, ALU.mult, ALU.add)
            self.sep(V)
            V.call("tensor_copy", self.junk[:, 0:64], self.junk[:, 64:128]).then_inc(vk, 1)
            C.wait(vk)
            C.call("activation", rs0, sd, AF.Sqrt)
            C.call("activation", self.junk[:, 64:128], self.junk[:, 0:64],
                   AF.Copy).then_inc(sc, 1)
            V.wait(sc)
            recip_newton(rs, rs0, r2, ee)
            V.call("tensor_scalar", xn1t[:, :], xn1t[:, :], rs, None, ALU.mult)
            V.call("engine_nop").then_inc(vk, 1)
            Y.wait(vk, vk.count)
            Y.call("dma_start", out=self.xn1_d[r0:r0 + 128, :],
                   in_=xn1t[:, :]).then_inc(st, 16)
            rms_quant(xn1t[:, :], 0)
            if "ctx_dbg" in self.dbg:
                V.call("engine_nop").then_inc(vk, 1)
                Y.wait(vk, vk.count)
                Y.call("dma_start", out=self.bq_dbg[r0:r0 + 128, :],
                       in_=xqt[:, :]).then_inc(st, 16)
                Y.call("dma_start", out=self.sx_dbg[r0:r0 + 128, :],
                       in_=stat[:, 0:1]).then_inc(st, 16)
                if ti == 0:
                    Y.call("dma_start", out=self.wq_dbg[:, :],
                           in_=wq_i[:, :]).then_inc(st, 16)
            qkv_tile((qb, kb, vb), (0, 1, 2), (wq_i, wk_i, wv_i))
            for o in range(16):
                n1 = 16 - o
                V.call("tensor_tensor", prodb[:, 0:n1 * 64], qb[:, 0:n1 * 64],
                       kb[:, o * 64:H], ALU.mult)
                V.call("tensor_reduce", sh[:, o * 16:o * 16 + n1],
                       prodb[:, 0:n1 * 64].rearrange("p (g d) -> p g d", d=64),
                       AX.X, ALU.add)
                if o > 0:
                    V.call("tensor_tensor", prodb[:, 0:o * 64], qb[:, n1 * 64:H],
                           kb[:, 0:o * 64], ALU.mult)
                    V.call("tensor_reduce", sh[:, o * 16 + n1:o * 16 + 16],
                           prodb[:, 0:o * 64].rearrange("p (g d) -> p g d", d=64),
                           AX.X, ALU.add)
            self.sep(V)
            V.call("tensor_scalar", sh[:, :], sh[:, :], 0.125, 0.0, ALU.mult, ALU.add)
            self.sep(V)
            V.call("tensor_reduce", rmax[:, :], sh3, AX.X, ALU.max)
            self.sep(V)
            for o in range(16):
                V.call("tensor_tensor", sh3[:, :, o:o + 1], sh3[:, :, o:o + 1],
                       rmax[:, :].rearrange("p (h u) -> p h u", u=1), ALU.subtract)
            V.call("tensor_copy", self.junk[:, 0:64], self.junk[:, 64:128]).then_inc(vk, 1)
            C.wait(vk)
            C.call("activation", sh[:, :], sh[:, :], AF.Exp)
            C.call("activation", self.junk[:, 64:128], self.junk[:, 0:64],
                   AF.Copy).then_inc(sc, 1)
            V.wait(sc)
            V.call("tensor_reduce", rsum[:, :], sh3, AX.X, ALU.add)
            self.sep(V)
            recip_newton(rmax[:, :], rsum[:, :], prodb[:, 0:16], prodb[:, 16:32])
            for hh in range(16):
                first = True
                for tt in range(16):
                    o = (tt - hh) % 16
                    pcol = sh[:, o * 16 + hh:o * 16 + hh + 1]
                    if first:
                        V.call("tensor_scalar", ctxt[:, hh * 64:(hh + 1) * 64],
                               vb[:, tt * 64:(tt + 1) * 64], pcol, None, ALU.mult)
                        first = False
                    else:
                        V.call("scalar_tensor_tensor", ctxt[:, hh * 64:(hh + 1) * 64],
                               vb[:, tt * 64:(tt + 1) * 64], pcol,
                               ctxt[:, hh * 64:(hh + 1) * 64], ALU.mult, ALU.add)
            self.sep(V)
            for hh in range(16):
                V.call("tensor_scalar", ctxt[:, hh * 64:(hh + 1) * 64],
                       ctxt[:, hh * 64:(hh + 1) * 64], rmax[:, hh:hh + 1], None, ALU.mult)
            V.call("engine_nop").then_inc(vk, 1)
            Y.wait(vk, vk.count)
            Y.call("dma_start", out=ctx_own[r0:r0 + 128, :], in_=ctxt[:, :]).then_inc(st, 16)
            if "ctx_dbg" in self.dbg:
                Y.call("dma_start", out=self.ctx_dbg[r0:r0 + 128, :],
                       in_=ctxt[:, :]).then_inc(st, 16)
                Y.call("dma_start", out=self.qkv_dbg[0, r0:r0 + 128, :],
                       in_=qb[:, :]).then_inc(st, 16)
                Y.call("dma_start", out=self.qkv_dbg[1, r0:r0 + 128, :],
                       in_=kb[:, :]).then_inc(st, 16)
                Y.call("dma_start", out=self.qkv_dbg[2, r0:r0 + 128, :],
                       in_=vb[:, :]).then_inc(st, 16)

        G.wait(st, st.count)
        G.call("collective_compute", "AllGather", ALU.bypass,
               [[0, 1], [2, 3], [4, 5], [6, 7]],
               [ctx_own.opt()], [ctx_pair.opt()]).then_inc(cc, 1)
        G.wait(cc, 1)
        for j in range(NTILE):
            src0 = (j % 2) * TOK + (j // 2) * 128
            G.call("dma_start", out=self.ctx_d[j * 128:(j + 1) * 128, :],
                   in_=ctx_pair[src0:src0 + 128, :]).then_inc(gd, 16)
        G.wait(gd, gd.count)
        p.barrier("p5ctx")
        self.free_scope("p5a")

        quant_weight(wq_i, "l1_woT", 3)
        scrA = self.sb("p5b_scrA", [128, H], F32, scope="p5a")
        scrB = self.sb("p5b_scrB", [128, H], F32, scope="p5a")
        scrt = self.sb("p5b_scrt", [128, H], F32, scope="p5a")
        a1t = self.sb("p5b_a1t", [128, H], F32, scope="p5a")
        xqT2 = self.sb("p5b_xqT", [128, SLAB * 128], F32, scope="p5a")
        parc = self.sb("p5b_parc", [128, 2], F32, scope="p5a")
        parf2 = self.sb("p5b_parf", [1, 1], F32, scope="p5a")
        V.call("tensor_copy", parf2[:, :], self.par_i[0:1, 0:1])
        self.sep(V)
        V.call("tensor_copy", self.junk[:, 0:64], self.junk[:, 64:128]).then_inc(vk, 1)
        T.wait(vk, vk.count)
        T.call("matmul", self.pD[:, 0:1], self.ones_col[:], parf2[:, :],
               start=True, stop=True, skip_group_check=True).then_inc(tk, 1)
        V.wait(tk)
        V.call("tensor_copy", parc[:, 0:1], self.pD[:, 0:1])
        self.sep(V)
        V.call("tensor_scalar", parc[:, 1:2], parc[:, 0:1], -1.0, 1.0, ALU.mult, ALU.add)
        self.sep(V)
        V.call("engine_nop").then_inc(vk, 1)

        def o_tile(ti):
            r0 = ti * 128
            Y.wait(vk, vk.count)
            Y.wait(st, st.count)
            Y.call("dma_start", out=scrA[:, :].rearrange("p (s c) -> p s c", s=16),
                   in_=self.ctx_d[0:TOK, 64 * ti:64 * ti + 64]
                   .rearrange("(p s) c -> p s c", s=16)).then_inc(ld, 16)
            Y.call("dma_start", out=scrB[:, :].rearrange("p (s c) -> p s c", s=16),
                   in_=self.ctx_d[TOK:S, 64 * ti:64 * ti + 64]
                   .rearrange("(p s) c -> p s c", s=16)).then_inc(ld, 16)
            V.wait(ld, ld.count)
            V.call("tensor_scalar", scrt[:, :], scrA[:, :], parc[:, 1:2], None, ALU.mult)
            V.call("scalar_tensor_tensor", scrt[:, :], scrB[:, :], parc[:, 0:1],
                   scrt[:, :], ALU.mult, ALU.add)
            V.call("engine_nop").then_inc(vk, 1)
            rms_quant(scrt[:, :], 0)
            V.call("tensor_copy", self.junk[:, 0:64], self.junk[:, 64:128]).then_inc(vk, 1)
            T.wait(vk, vk.count)
            for b0 in range(0, SLAB, 4):
                T.wait(vk, vk.count)
                for j in range(b0, b0 + 4):
                    h = self.tr(T, self.pA[:, (j - b0) * 128:(j - b0 + 1) * 128],
                                xqt[:, j * 128:(j + 1) * 128])
                h.then_inc(tk, 1)
                V.wait(tk)
                for j in range(b0, b0 + 4):
                    V.call("tensor_copy", xqT2[:, j * 128:(j + 1) * 128],
                           self.pA[:, (j - b0) * 128:(j - b0 + 1) * 128])
                V.call("engine_nop").then_inc(vk, 1)
            T.wait(vk, vk.count)
            for nh in range(2):
                for ks in range(SLAB):
                    hm = T.call("matmul", self.pE[:, nh * 512:(nh + 1) * 512],
                                xqT2[:, ks * 128:(ks + 1) * 128],
                                wq_i[:, ks * H + nh * 512:ks * H + (nh + 1) * 512],
                                start=(ks == 0), stop=(ks == SLAB - 1),
                                skip_group_check=True)
            hm.then_inc(tk, 1)
            V.wait(tk)
            V.call("tensor_tensor", stat[:, 8:9], stat[:, 0:1], swcol[:, 3:4], ALU.mult)
            self.sep(V)
            for nh in range(2):
                V.call("tensor_scalar", a1t[:, nh * 512:(nh + 1) * 512],
                       self.pE[:, nh * 512:(nh + 1) * 512], stat[:, 8:9], None, ALU.mult)
            V.call("engine_nop").then_inc(vk, 1)
            Y.wait(vk, vk.count)
            Y.call("dma_start", out=self.a1_d[r0:r0 + 128, :], in_=a1t[:, :]).then_inc(st, 16)

        for ti in range(NBLK):
            o_tile(ti)
        self.free_scope("p5a")

        TC = 256
        NCH = TOK // TC
        N24 = L1I // 128
        xn1T = self.sb("p5c_xn1T", [128, SLAB * TC], F32, scope="p5a")
        Hf = self.sb("p5c_Hf", [128, N24 * TC], F32, scope="p5a")
        wgfp = self.sb("p5c_wg", [128, SLAB * 128], F32, scope="p5a")
        wufp = self.sb("p5c_wu", [128, SLAB * 128], F32, scope="p5a")
        wgi = self.sb("p5c_wgi", [128, SLAB * 128], I32, scope="p5a")
        wui = self.sb("p5c_wui", [128, SLAB * 128], I32, scope="p5a")
        Hgb = self.sb("p5c_Hg", [128, TC], F32, scope="p5a")
        otile = self.sb("p5c_ot", [128, H], F32, scope="p5a")
        x1tl = self.sb("p5c_x1", [128, H], F32, scope="p5a")
        a1tl = self.sb("p5c_a1", [128, H], F32, scope="p5a")

        def wdsl(ks):
            buf = (wq_i, wk_i, wv_i)[ks // 8]
            k2 = ks % 8
            return buf[:, k2 * H:(k2 + 1) * H]
        Y.wait(vk, vk.count)
        Y.wait(st, st.count)
        for ks in range(N24):
            Y.wait(vk, vk.count)
            Y.call("dma_start", out=ti32[:, :],
                   in_=self.w["l1_wdT"][ks * 128:(ks + 1) * 128, :]).then_inc(ld, 16)
            V.wait(ld, ld.count)
            V.call("tensor_copy", wdsl(ks), ti32[:, :])
            V.call("engine_nop").then_inc(vk, 1)

        if "ctx_dbg" in self.dbg:
            Y.wait(vk, vk.count)
            for bi, buf in enumerate((wq_i, wk_i, wv_i)):
                Y.call("dma_start", out=self.wd_dbg[bi, :, :], in_=buf[:, :]).then_inc(st, 16)
            V.wait(st, st.count)
        for tch in range(NCH):
            t0 = tch * TC
            for tt in range(TC // 128):
                Y.wait(vk, vk.count)
                Y.call("dma_start", out=x1tl[:, :],
                       in_=self.xn1_d[t0 + tt * 128:t0 + (tt + 1) * 128, :]).then_inc(ld, 16)
                T.wait(ld, ld.count)
                for b0 in range(0, SLAB, 4):
                    T.wait(vk, vk.count)
                    for j in range(b0, b0 + 4):
                        h = self.tr(T, self.pA[:, (j - b0) * 128:(j - b0 + 1) * 128],
                                    x1tl[:, j * 128:(j + 1) * 128])
                    h.then_inc(tk, 1)
                    V.wait(tk)
                    for j in range(b0, b0 + 4):
                        V.call("tensor_copy", xn1T[:, j * TC + tt * 128:j * TC + (tt + 1) * 128],
                               self.pA[:, (j - b0) * 128:(j - b0 + 1) * 128])
                    V.call("engine_nop").then_inc(vk, 1)
            for mch in range(N24):
                Y.wait(vk, vk.count)
                Y.wait(tk, tk.count)
                Y.call("dma_start", out=wgi[:, :].rearrange("p (s c) -> p s c", s=SLAB),
                       in_=self.w["l1_wgT"][:, mch * 128:(mch + 1) * 128]
                       .rearrange("(s p) c -> p s c", p=128)).then_inc(ld, 16)
                Y.call("dma_start", out=wui[:, :].rearrange("p (s c) -> p s c", s=SLAB),
                       in_=self.w["l1_wuT"][:, mch * 128:(mch + 1) * 128]
                       .rearrange("(s p) c -> p s c", p=128)).then_inc(ld, 16)
                V.wait(ld, ld.count)
                V.call("tensor_copy", wgfp[:, :], wgi[:, :])
                V.call("tensor_copy", wufp[:, :], wui[:, :])
                V.call("engine_nop").then_inc(vk, 1)
                T.wait(vk, vk.count)
                for ks in range(SLAB):
                    T.call("matmul", self.pE[:, 0:TC], wgfp[:, ks * 128:(ks + 1) * 128],
                           xn1T[:, ks * TC:(ks + 1) * TC],
                           start=(ks == 0), stop=(ks == SLAB - 1), skip_group_check=True)
                    hu = T.call("matmul", self.pE[:, 512:512 + TC],
                                wufp[:, ks * 128:(ks + 1) * 128],
                                xn1T[:, ks * TC:(ks + 1) * TC],
                                start=(ks == 0), stop=(ks == SLAB - 1),
                                skip_group_check=True)
                hu.then_inc(tk, 1)
                C.wait(tk)
                C.call("activation", Hgb[:, :], self.pE[:, 0:TC], AF.Silu, scale=S127)
                C.call("activation", self.junk[:, 64:128], self.junk[:, 0:64],
                       AF.Copy).then_inc(sc, 1)
                V.wait(sc)
                V.call("scalar_tensor_tensor", Hf[:, mch * TC:(mch + 1) * TC],
                       self.pE[:, 512:512 + TC], S127, Hgb[:, :], ALU.mult, ALU.mult)
                if "ctx_dbg" in self.dbg and tch == 0 and mch == 0:
                    V.call("tensor_copy", tf32[:, 0:TC], self.pE[:, 0:TC])
                    V.call("tensor_copy", tf32[:, 512:512 + TC], self.pE[:, 512:512 + TC])
                V.call("engine_nop").then_inc(vk, 1)
                if "ctx_dbg" in self.dbg and tch == 0 and mch == 0:
                    Y.wait(vk, vk.count)
                    Y.call("dma_start", out=self.gu_dbg[0, :, 0:TC],
                           in_=tf32[:, 0:TC]).then_inc(st, 16)
                    Y.call("dma_start", out=self.gu_dbg[1, :, 0:TC],
                           in_=tf32[:, 512:512 + TC]).then_inc(st, 16)
                    V.wait(st, st.count)
                T.wait(vk, vk.count)
            if "ctx_dbg" in self.dbg and tch == 0:
                Y.wait(vk, vk.count)
                Y.call("dma_start", out=self.hf_dbg[:, :], in_=Hf[:, :]).then_inc(st, 16)
                V.wait(st, st.count)
            for tt in range(TC // 128):
                r0 = t0 + tt * 128
                Y.wait(vk, vk.count)
                Y.call("dma_start", out=x1tl[:, :],
                       in_=self.x1_d[r0:r0 + 128, :]).then_inc(ld, 16)
                Y.call("dma_start", out=a1tl[:, :],
                       in_=self.a1_d[r0:r0 + 128, :]).then_inc(ld, 16)
                T.wait(ld, ld.count)
                for nh in range(2):
                    for ks in range(N24):
                        hd = T.call("matmul", self.pE[:, nh * 512:(nh + 1) * 512],
                                    Hf[:, ks * TC + tt * 128:ks * TC + (tt + 1) * 128],
                                    wdsl(ks)[:, nh * 512:(nh + 1) * 512],
                                    start=(ks == 0), stop=(ks == N24 - 1),
                                    skip_group_check=True)
                hd.then_inc(tk, 1)
                V.wait(tk)
                if "ctx_dbg" in self.dbg and tch == 0 and tt == 0:
                    V.call("tensor_copy", tf32[:, :], self.pE[:, 0:1024])
                    V.call("engine_nop").then_inc(vk, 1)
                    Y.wait(vk, vk.count)
                    Y.call("dma_start", out=self.dpe_dbg[:, :],
                           in_=tf32[:, :]).then_inc(st, 16)
                    V.wait(st, st.count)
                for nh in range(2):
                    V.call("tensor_scalar", otile[:, nh * 512:(nh + 1) * 512],
                           self.pE[:, nh * 512:(nh + 1) * 512], S127, 0.0,
                           ALU.mult, ALU.add)
                V.call("tensor_tensor", otile[:, :], otile[:, :], x1tl[:, :], ALU.add)
                V.call("tensor_tensor", otile[:, :], otile[:, :], a1tl[:, :], ALU.add)
                V.call("engine_nop").then_inc(vk, 1)
                Y.wait(vk, vk.count)
                Y.call("dma_start", out=self.out_own[r0:r0 + 128, :],
                       in_=otile[:, :]).then_inc(st, 16)
                T.wait(st, st.count)
        Y.wait(st, st.count)
        V.wait(st, st.count)
        self.free_scope("p5a")
        self.free_scope("p5")
        p.barrier("p5")


def build_program(dbg=(), upto=None):
    nc = bass.Bass("TRN2", target_bir_lowering=False, detect_race_conditions=False)
    stack = ExitStack()
    p = Prog(nc, stack)
    b = Builder(nc, p, dbg)
    b.psum_pool()
    b.p0()
    if upto != "p0":
        b.p1()
    if upto not in ("p0", "p1"):
        b.p2(upto)
    if upto is None or upto in ("p2cmax", "p2ln", "p2ssm", "p3"):
        b.p2b(upto if upto != "p3" else None)
    if upto is None or upto in ("p3", "p4"):
        b.p3()
    if upto is None or upto in ("p4", "p5"):
        b.p4()
    if upto is None or upto == "p5":
        b.p5()
    if upto != "p0" and "xB_dbg" in b.dbg:
        xbd = b.dram("xB_dbg", [16, S], F32)
        sxb = p.sem("sxb")
        p.s["sync"].call("dma_start", out=xbd[:, :], in_=b.xB_T[:, 0:S]).then_inc(sxb, 16)
        p.s["sync"].wait(sxb, 16)

    if upto is not None and upto != "p5":
        st = p.sem("st_out")
        zb = b.sb("zb", [128, H], F32, scope="p2")
        fin = p.sem("fin")
        p.s["vector"].call("memset", zb[:], 0.0).then_inc(fin, 1)
        p.s["sync"].wait(fin, 1)
        for i in range(NBLK):
            p.s["sync"].call("dma_start", out=b.out_own[i * 128:(i + 1) * 128, :],
                             in_=zb[:]).then_inc(st, 16)
        p.s["sync"].wait(st, st.count)

    with stack:
        with nc.Block() as block:
            for e in ENGINES:
                getattr(block, e)(lambda eng, _e=e: p.s[_e].replay(eng))
    return nc, b


def make_in_maps(inputs):
    xs = np.ascontiguousarray(np.asarray(inputs["L0_attn_x"], dtype=np.float32))
    in_maps = []
    for c in range(8):
        sq, par = c // 2, c % 2
        blocks = [i for i in range(32) if i % 2 == par]
        own_rows = np.concatenate([np.arange(i * 128, (i + 1) * 128) for i in blocks])
        im = {
            "xseq": xs[sq],
            "x_own": np.ascontiguousarray(xs[sq][own_rows]),
            "pos_own": own_rows.astype(np.float32)[None, :],
            "parity": np.array([[par]], dtype=np.int32),
            "own_idx": own_rows.astype(np.uint32)[:, None],
        }
        for nm, shp, dt in WEIGHT_SPECS:
            a = np.asarray(inputs[nm])
            npdt = np.int32 if dt is I32 else np.float32
            im["w_" + nm] = np.ascontiguousarray(a.reshape(shp).astype(npdt))
        rg = np.asarray(inputs["L0_ffn_routed_gate"], np.float32)
        ru = np.asarray(inputs["L0_ffn_routed_up"], np.float32)
        rd = np.asarray(inputs["L0_ffn_routed_down"], np.float32)
        sg = np.asarray(inputs["L0_ffn_shared_gate"], np.float32).reshape(1, IM, H)
        su = np.asarray(inputs["L0_ffn_shared_up"], np.float32).reshape(1, IM, H)
        sd = np.asarray(inputs["L0_ffn_shared_down"], np.float32).reshape(1, H, IM)
        im["w_moe_gT"] = np.ascontiguousarray(
            np.concatenate([rg, sg], 0).transpose(0, 2, 1))
        im["w_moe_uT"] = np.ascontiguousarray(
            np.concatenate([ru, su], 0).transpose(0, 2, 1))
        im["w_moe_dT"] = np.ascontiguousarray(
            np.concatenate([rd, sd], 0).transpose(0, 2, 1))
        im["w_rw_T"] = np.ascontiguousarray(
            np.asarray(inputs["L0_ffn_router_w"], np.float32).T.reshape(SLAB, 128, NE))
        im["w_r_bias"] = np.ascontiguousarray(
            np.asarray(inputs["L0_ffn_routing_bias"], np.float32).reshape(NE, 1))
        for nm2 in ("wq", "wk", "wv", "wo"):
            im["w_l1_" + nm2 + "T"] = np.ascontiguousarray(
                np.asarray(inputs["L1_attn_" + nm2], np.float32).T)
        im["w_l1_wgT"] = np.ascontiguousarray(
            np.asarray(inputs["L1_ffn_w_gate"], np.int32).T)
        im["w_l1_wuT"] = np.ascontiguousarray(
            np.asarray(inputs["L1_ffn_w_up"], np.int32).T)
        im["w_l1_wdT"] = np.ascontiguousarray(
            np.asarray(inputs["L1_ffn_w_down"], np.int32).T)
        im["w_ln2_col"] = np.ascontiguousarray(
            np.asarray(inputs["L0_ffn_ln2_w"], np.float32).reshape(SLAB, 128).T)
        in_maps.append(im)
    return in_maps


def kernel(**inputs):
    nc, b = build_program()
    in_maps = make_in_maps(inputs)
    res = run_bass_kernel_spmd(nc, in_maps, core_ids=list(range(8)))
    out = np.zeros((B, S, H), dtype=np.float32)
    for c in range(8):
        sq, par = c // 2, c % 2
        blocks = [i for i in range(32) if i % 2 == par]
        own_rows = np.concatenate([np.arange(i * 128, (i + 1) * 128) for i in blocks])
        out[sq][own_rows] = res.results[c]["out_own"]
    return out
